# version 21
# baseline (speedup 1.0000x reference)
import numpy as np
from contextlib import ExitStack

import concourse.bass as bass
import concourse.mybir as mybir
from concourse.bass_utils import run_bass_kernel_spmd

F32 = mybir.dt.float32
BF16 = mybir.dt.bfloat16
I32 = mybir.dt.int32
AF = mybir.ActivationFunctionType
ALU = mybir.AluOpType
AX = mybir.AxisListType

T, D, H, HD, U = 2048, 1024, 16, 64, 38
NT, NI, NP = T // 128, D // 128, H // 2
f32 = np.float32
pf = float


class Ctr:
    def __init__(self, h):
        self.h, self.n = h, 0

    def inc(self, instr, k=1):
        instr.then_inc(self.h, k)
        self.n += k
        return instr


class Region:
    def __init__(self, nc):
        self.nc, self.cms = nc, []

    def sb(self, name, shp, dt=F32, side=None):
        cm = self.nc.sbuf_tensor(name, shp, dt, side=side) if side else self.nc.sbuf_tensor(name, shp, dt)
        h = cm.__enter__()
        self.cms.append(cm)
        return h

    def ps(self, name, shp):
        cm = self.nc.psum_tensor(name, shp, F32)
        h = cm.__enter__()
        self.cms.append(cm)
        return h

    def close(self):
        for cm in reversed(self.cms):
            cm.__exit__(None, None, None)
        self.cms = []


def build_nc(taps=(), stop_after=None):
    nc = bass.Bass("TRN2", target_bir_lowering=False,
                   detect_race_conditions=False)

    x_h = nc.dram_tensor("x", [T, D], F32, kind="ExternalInput")
    w_h = {n: nc.dram_tensor(n, [D, D], F32, kind="ExternalInput")
           for n in ("Wq", "Wk", "Wv", "Wo", "Wf")}
    b_h = {n: nc.dram_tensor(n, [D], F32, kind="ExternalInput")
           for n in ("bq", "bk", "bv", "bo", "gamma", "beta")}
    y_h = nc.dram_tensor("y", [T, D], F32, kind="ExternalOutput")
    cc_in = nc.dram_tensor("cc_in", [1, 8], F32)
    cc_out = nc.dram_tensor("cc_out", [1, 8], F32)
    x_ap, y_ap = x_h.ap(), y_h.ap()

    tap_aps = {}
    def mktap(name, shp, dt=F32):
        if name in taps:
            tap_aps[name] = nc.dram_tensor("tap_" + name, shp, dt,
                                           kind="ExternalOutput").ap()
        return tap_aps.get(name)

    def row(h):
        return h.ap().rearrange("(a d) -> a d", a=1)

    st = ExitStack()
    perm = Region(nc)
    st.callback(perm.close)

    stopped = [False]


    sD = Ctr(st.enter_context(nc.semaphore("sD")))
    sG = Ctr(st.enter_context(nc.semaphore("sG")))
    sT = Ctr(st.enter_context(nc.semaphore("sT")))
    sV = Ctr(st.enter_context(nc.semaphore("sV")))
    sS = Ctr(st.enter_context(nc.semaphore("sS")))
    sC = Ctr(st.enter_context(nc.semaphore("sC")))
    sW = Ctr(st.enter_context(nc.semaphore("sW")))
    sF = [Ctr(st.enter_context(nc.semaphore(f"sF{i}"))) for i in range(2)]
    sX = [Ctr(st.enter_context(nc.semaphore(f"sX{i}"))) for i in range(2)]
    sO = [Ctr(st.enter_context(nc.semaphore(f"sO{i}"))) for i in range(2)]
    sCC = Ctr(st.enter_context(nc.semaphore("sCC")))

    def _stop(point, waits, dumps):
        if stop_after != point:
            return False
        stopped[0] = True
        with nc.Block() as blk:
            @blk.gpsimd
            def _(e):
                for h, v in waits:
                    e.wait_ge(h, v)
                for src, r0 in dumps:
                    rows, cols = src.shape[0], int(np.prod(src.shape[1:]))
                    sG.inc(e.dma_start(out=y_ap[r0:r0+rows, 0:cols], in_=src,
                                       allow_slow_non_contiguous=True), 16)
                e.wait_ge(sG.h, sG.n)
        return True

    i128b = perm.sb("i128b", [128, 128], BF16)
    i128f = perm.sb("i128f", [128, 128], F32)
    ones_rb = perm.sb("ones_rb", [128, 512], BF16)
    ones_cb = perm.sb("ones_cb", [128, 1], BF16)
    ones_cf = perm.sb("ones_cf", [128, 1], F32)
    ones128b = perm.sb("ones128b", [128, 128], BF16)
    ones_rf = perm.sb("ones_rf", [128, 128], F32)
    iota_r = perm.sb("iota_r", [128, 128], F32)
    iota_c = perm.sb("iota_c", [128, 1], F32)
    iota2d = perm.sb("iota2d", [128, 16], F32)
    iota38 = perm.sb("iota38", [128, 38], F32)
    iotaC2 = perm.sb("iotaC2", [128, 32], BF16)
    iotaC2f = perm.sb("iotaC2f", [128, 32], F32)

    rowpk = perm.sb("rowpk", [128, 1024], F32)
    R_GA, R_BE, R_BO, R_IDX, R_RECIP = 0, 32, 64, 0, 0
    rowpb = perm.sb("rowpb", [128, 1024], BF16)
    cc_sb = perm.sb("cc_sb", [1, 8], F32)
    srow = perm.sb("srow", [1, 16], F32)
    B_BQ, B_BK, B_BV = 0, 32, 64

    st16 = perm.sb("st16", [128, 96])
    SL = lambda k: st16[:, 16 * k:16 * (k + 1)]
    S_SUM, S_SSQ, S_MU, S_RSTD, S_NMR, S_TMP = range(6)
    amax = perm.sb("amax", [128, 16])
    inv_col = perm.sb("inv_col", [128, 1])
    sa_col = perm.sb("sa_col", [128, 1])
    dcol = perm.sb("dcol", [128, 1])
    wstat = perm.sb("wstat", [128, 24])
    idxcol = perm.sb("idxcol", [128, 8])
    idx_bc = perm.sb("idx_bc", [128, 1024])
    mx8 = perm.sb("mx8", [16, 48])
    maskT = perm.sb("maskT", [128, 256], BF16)
    rank_sb = perm.sb("rank_sb", [128, 256])
    qselT = [perm.sb(f"qselT{j}", [128, 76], BF16) for j in range(NP)]
    soT = [perm.sb(f"soT{j}", [128, 76], BF16) for j in range(NP)]
    fscr = [perm.sb(f"fscr{r}", [128, 1024]) for r in range(2)]
    qiT = perm.sb("qiT", [128, NI * 2048], BF16)

    rA = Region(nc)
    arena = [rA.sb(f"ar{m}", [128, 1024]) for m in range(NT)]
    rGB = Region(nc)
    gb_bc = rGB.sb("gb_bc", [128, 2048])

    x1 = [0] * NT
    with nc.Block() as blk:
        @blk.sync
        def _(e):
            for m in range(NT):
                sD.inc(e.dma_start(out=arena[m][:], in_=x_ap[128*m:128*(m+1), :]), 16)
                x1[m] = sD.n
            sD.inc(e.dma_start(out=rowpk[R_GA:R_GA+1, :], in_=row(b_h["gamma"])), 16)
            sD.inc(e.dma_start(out=rowpk[R_BE:R_BE+1, :], in_=row(b_h["beta"])), 16)
            sD.inc(e.dma_start(out=rowpk[R_BO:R_BO+1, :], in_=row(b_h["bo"])), 16)
        d_rows = sD.n

        @blk.gpsimd
        def _(e):
            kw = dict(allow_small_or_imprecise_dtypes=True)
            e.iota(iota_r[:], [[1, 128]], base=0, channel_multiplier=0, **kw)
            e.iota(iota_c[:], [[1, 1]], base=0, channel_multiplier=1, **kw)
            e.iota(iota2d[:], [[128, 16]], base=0, channel_multiplier=1, **kw)
            e.iota(iota38[:], [[1, 38]], base=1, channel_multiplier=0, **kw)
            e.iota(iotaC2f[:, 0:32:2], [[0, 16]], base=0, channel_multiplier=1, **kw)
            e.iota(iotaC2f[:, 1:32:2], [[1, 16]], base=0, channel_multiplier=0, **kw)
            e.memset(ones_rb[:], 1.0)
            e.memset(ones_cb[:], 1.0)
            e.memset(ones_cf[:], 1.0)
            e.memset(ones128b[:], 1.0)
            e.memset(ones_rf[:], 1.0)
            for j in range(NP):
                e.memset(qselT[j][:], 0.0)
                e.memset(soT[j][:], 0.0)
            e.memset(cc_sb[:], 0.0)
            sW.inc(e.memset(idxcol[:], -1.0))
            sG.inc(e.dma_start(out=rowpb[B_BQ:B_BQ+1, :], in_=row(b_h["bq"])), 16)
            sG.inc(e.dma_start(out=rowpb[B_BK:B_BK+1, :], in_=row(b_h["bk"])), 16)
            sG.inc(e.dma_start(out=rowpb[B_BV:B_BV+1, :], in_=row(b_h["bv"])), 16)
        g_consts = sW.n

        @blk.vector
        def _(e):
            e.wait_ge(sW.h, g_consts)
            e.tensor_scalar(i128b[:], iota_r[:], iota_c[:], None, ALU.is_equal)
            e.tensor_scalar(i128f[:], iota_r[:], iota_c[:], None, ALU.is_equal)
            sV.inc(e.tensor_copy(iotaC2[:], iotaC2f[:]))
        v_consts = sV.n

    r1 = Region(nc)
    pB = [r1.ps(f"pB{r}", [128, 512]) for r in range(4)]
    with nc.Block() as blk:
        @blk.tensor
        def _(e):
            e.wait_ge(sD.h, d_rows)
            e.wait_ge(sW.h, g_consts)
            k = 0
            for ri in (R_GA, R_BE):
                for n in range(2):
                    sT.inc(e.matmul(pB[k][:, 0:512], ones_rf[ri:ri+1, :],
                                    rowpk[ri:ri+1, 512*n:512*(n+1)],
                                    start=True, stop=True))
                    k += 1
        t_sp1 = sT.n

        @blk.vector
        def _(e):
            e.wait_ge(sT.h, t_sp1)
            e.tensor_copy(gb_bc[:, 0:512], pB[0][:, 0:512])
            e.tensor_copy(gb_bc[:, 512:1024], pB[1][:, 0:512])
            e.tensor_copy(gb_bc[:, 1024:1536], pB[2][:, 0:512])
            sV.inc(e.tensor_copy(gb_bc[:, 1536:2048], pB[3][:, 0:512]))
        v_sp1 = sV.n
    r1.close()

    FBLK_cm = nc.Block()
    FBLK = FBLK_cm.__enter__()
    if True:
        blk = FBLK
        @blk.vector
        def _(e):
            for m in range(NT):
                e.wait_ge(sD.h, x1[m])
                instr = e.tensor_reduce(SL(S_SUM)[:, m:m+1], arena[m][:], AX.X, ALU.add)
            sV.inc(instr)
            e.wait_ge(sV.h, sV.n)
            sV.inc(e.tensor_scalar(SL(S_MU), SL(S_SUM), pf(1.0 / D), None, ALU.mult))
            e.wait_ge(sV.h, sV.n)
            sV.inc(e.tensor_tensor(SL(S_TMP), SL(S_MU), SL(S_MU), ALU.mult))
        v_a = sV.n

        @blk.scalar
        def _(e):
            for m in range(NT):
                e.wait_ge(sD.h, x1[m])
                e.activation(fscr[1][:], arena[m][:], AF.Square,
                             accum_out=SL(S_SSQ)[:, m:m+1])
            sS.inc(e.copy(fscr[1][:, 2:3], arena[0][:, 0:1]))
        s_a = sS.n

        @blk.vector
        def _(e):
            e.wait_ge(sS.h, s_a)
            sV.inc(e.scalar_tensor_tensor(SL(S_SSQ), SL(S_SSQ), pf(1.0 / D), SL(S_TMP),
                                          ALU.mult, ALU.subtract))
            e.wait_ge(sV.h, sV.n)
            sV.inc(e.tensor_scalar(SL(S_SSQ), SL(S_SSQ), pf(1e-5), None, ALU.add))
        v_b = sV.n

        @blk.scalar
        def _(e):
            e.wait_ge(sV.h, v_b)
            sS.inc(e.activation(SL(S_TMP), SL(S_SSQ), AF.Sqrt))
        s_b = sS.n

        @blk.vector
        def _(e):
            e.wait_ge(sS.h, s_b)
            sV.inc(e.reciprocal(SL(S_RSTD), SL(S_TMP)))
            e.wait_ge(sV.h, sV.n)
            sV.inc(e.scalar_tensor_tensor(SL(S_NMR), SL(S_MU), pf(-1.0), SL(S_RSTD),
                                          ALU.mult, ALU.mult))
        v_stats1 = sV.n

    if _stop("stats", [(sV.h, v_stats1), (sS.h, s_a)],
             [(st16[:], 0), (arena[3][:], 128), (arena[0][:], 256)]):
        return nc, tap_aps

    rXB = Region(nc)
    xnb = rXB.sb("xnb", [128, NT * 1024], BF16)
    if True:
        blk = FBLK
        @blk.scalar
        def _(e):
            e.wait_ge(sV.h, v_stats1)
            for m in range(NT):
                sS.inc(e.activation(arena[m][:], arena[m][:], AF.Identity,
                                    scale=SL(S_RSTD)[:, m:m+1], bias=SL(S_NMR)[:, m:m+1]))
        s_xn0 = sS.n - NT

        @blk.vector
        def _(e):
            for m in range(NT):
                e.wait_ge(sS.h, s_xn0 + m + 1)
                sV.inc(e.tensor_copy(xnb[:, 1024*m:1024*(m+1)], arena[m][:]))
        v_xnb0 = sV.n - NT

    if _stop("xn", [(sS.h, s_xn0 + NT)],
             [(arena[m][:], 128 * m) for m in range(NT)]):
        return nc, tap_aps

    if mktap("xn", [128, 1024]):
        with nc.Block() as blk:
            @blk.sync
            def _(e):
                e.wait_ge(sS.h, s_xn0 + 4)
                sD.inc(e.dma_start(out=tap_aps["xn"], in_=arena[3][:]), 16)

    rXT = Region(nc)
    xnT = rXT.sb("xnT", [128, NI * 2048], BF16, side="right")
    r6 = Region(nc)
    pT6 = [r6.ps(f"pT6{r}", [128, 512]) for r in range(2)]
    if True:
        blk = FBLK
        t6, s6 = sT.n, sS.n

        @blk.tensor
        def _(e):
            e.wait_ge(sV.h, v_sp1)
            g = 0
            for m in range(NT):
                e.wait_ge(sV.h, v_xnb0 + m + 1)
                for a in range(2):
                    if g >= 2:
                        e.wait_ge(sS.h, s6 + (g - 2) + 1)
                    for i in range(4):
                        ic = 4 * a + i
                        instr = e.matmul(pT6[g % 2][:, 128*i:128*(i+1)],
                                         xnb[:, 1024*m + 128*ic:1024*m + 128*(ic+1)],
                                         i128b[:], start=True, stop=True)
                        if i == 3:
                            sT.inc(instr)
                    g += 1

        @blk.scalar
        def _(e):
            for g in range(2 * NT):
                m, a = g // 2, g % 2
                e.wait_ge(sT.h, t6 + g + 1)
                dst = xnT[:].rearrange("p (c t) -> p c t", t=2048)[:, 4*a:4*a+4, 128*m:128*(m+1)]
                src = pT6[g % 2][:].rearrange("p (c t) -> p c t", t=128)
                sS.inc(e.copy(dst, src))
        s_xnT = sS.n
    r6.close()
    if _stop("xnT", [(sS.h, s_xnT)],
             [(xnT[:, 1024*k:1024*(k+1)], 128 * k) for k in range(16)]):
        return nc, tap_aps
    rXB.close()


    if True:
        blk = FBLK
        @blk.vector
        def _(e):
            for m in range(NT):
                e.wait_ge(sS.h, s_xn0 + m + 1)
                instr = e.tensor_reduce(SL(S_SUM)[:, m:m+1], arena[m][:], AX.X, ALU.add)
            sV.inc(instr)
            e.wait_ge(sV.h, sV.n)
            sV.inc(e.tensor_scalar(SL(S_MU), SL(S_SUM), pf(1.0 / D), None, ALU.mult))
            e.wait_ge(sV.h, sV.n)
            sV.inc(e.tensor_scalar(SL(S_SUM), SL(S_SUM), pf(-1.0 / D), None, ALU.mult))
        v_c = sV.n

        @blk.scalar
        def _(e):
            e.wait_ge(sV.h, v_c)
            for m in range(NT):
                e.activation(fscr[1][:], arena[m][:], AF.Square,
                             bias=SL(S_SUM)[:, m:m+1],
                             accum_out=SL(S_SSQ)[:, m:m+1])
            sS.inc(e.copy(fscr[1][:, 2:3], arena[0][:, 0:1]))
        s_c = sS.n

        @blk.vector
        def _(e):
            e.wait_ge(sS.h, s_c)
            sV.inc(e.tensor_scalar(SL(S_SSQ), SL(S_SSQ), pf(1.0 / D), pf(1e-5),
                                   ALU.mult, ALU.add))
        v_d = sV.n

        @blk.scalar
        def _(e):
            e.wait_ge(sV.h, v_d)
            sS.inc(e.activation(SL(S_TMP), SL(S_SSQ), AF.Sqrt))
        s_d = sS.n

        @blk.vector
        def _(e):
            e.wait_ge(sS.h, s_d)
            sV.inc(e.reciprocal(SL(S_RSTD), SL(S_TMP)))
            e.wait_ge(sV.h, sV.n)
            sV.inc(e.scalar_tensor_tensor(SL(S_NMR), SL(S_MU), pf(-1.0), SL(S_RSTD),
                                          ALU.mult, ALU.mult))
        v_stats2 = sV.n

        @blk.scalar
        def _(e):
            e.wait_ge(sV.h, v_stats2)
            for m in range(NT):
                sS.inc(e.activation(arena[m][:], arena[m][:], AF.Identity,
                                    scale=SL(S_RSTD)[:, m:m+1], bias=SL(S_NMR)[:, m:m+1]))
        s_s1 = sS.n - NT

        @blk.vector
        def _(e):
            e.wait_ge(sV.h, v_sp1)
            for m in range(NT):
                e.wait_ge(sS.h, s_s1 + m + 1)
                e.tensor_tensor(arena[m][:], arena[m][:], gb_bc[:, 0:1024], ALU.mult)
                e.tensor_tensor(arena[m][:], arena[m][:], gb_bc[:, 1024:2048], ALU.add)
                sV.inc(e.tensor_reduce(amax[:, m:m+1], arena[m][:], AX.X, ALU.max,
                                       apply_absolute_value=True))
        v_xn2 = sV.n

    if _stop("xn2", [(sV.h, v_xn2)],
             [(arena[m][:], 128 * m) for m in range(NT)]):
        return nc, tap_aps
    rGB.close()
    rKTa = Region(nc)
    kTa = rKTa.sb("kTa", [128, 3 * 2048], BF16, side="right")
    rWK = Region(nc)
    wkb = rWK.sb("wkb", [128, NI * 1024], BF16, side="right")
    rWQ = Region(nc)
    wqb = rWQ.sb("wqb", [128, NI * 1024], BF16, side="right")
    if True:
        blk = FBLK
        @blk.gpsimd
        def _(e):
            e.wait_ge(sS.h, s_xnT)
            for i in range(NI):
                sG.inc(e.dma_start(out=wkb[:, 1024*i:1024*(i+1)],
                                   in_=w_h["Wk"].ap()[128*i:128*(i+1), :]), 16)
        g_wk = sG.n

        @blk.gpsimd
        def _(e):
            for i in range(NI):
                sG.inc(e.dma_start(out=wqb[:, 1024*i:1024*(i+1)],
                                   in_=w_h["Wq"].ap()[128*i:128*(i+1), :]), 16)
        g_wq = sG.n


    if mktap("xn2", [128, 1024]):
        with nc.Block() as blk:
            @blk.sync
            def _(e):
                e.wait_ge(sV.h, v_xn2)
                sD.inc(e.dma_start(out=tap_aps["xn2"], in_=arena[3][:]), 16)

    r8 = Region(nc)
    p8a = r8.ps("p8a", [1, 128])
    p8b = r8.ps("p8b", [128, 1])
    if True:
        blk = FBLK
        @blk.vector
        def _(e):
            sV.inc(e.tensor_reduce(SL(S_TMP)[:, 0:1], amax[:], AX.X, ALU.max))
        v_e = sV.n

        @blk.tensor
        def _(e):
            e.wait_ge(sV.h, v_e)
            e.wait_ge(sS.h, s_xnT)
            sT.inc(e.matmul(p8a[:], SL(S_TMP)[:, 0:1], i128f[:], start=True, stop=True))
        t_e = sT.n

        @blk.vector
        def _(e):
            e.wait_ge(sT.h, t_e)
            sV.inc(e.tensor_reduce(srow[0:1, 0:1], p8a[0:1, :], AX.X, ALU.max))
            e.wait_ge(sV.h, sV.n)
            sV.inc(e.tensor_copy(cc_sb[0:1, 0:1], srow[0:1, 0:1]))
            e.wait_ge(sV.h, sV.n)
        v_f = sV.n

        @blk.sync
        def _(e):
            e.wait_ge(sV.h, v_f)
            sCC.inc(e.dma_start(out=cc_in.ap(), in_=cc_sb[:]), 16)

        @blk.gpsimd
        def _(e):
            e.wait_ge(sCC.h, 16)
            if stop_after == "ccin":
                sG.inc(e.dma_start(out=fscr[0][0:1, 0:128], in_=cc_in.ap()), 16)
                e.wait_ge(sG.h, sG.n)
                sG.inc(e.dma_start(out=y_ap[0:1, 0:128], in_=fscr[0][0:1, 0:128]), 16)
                sG.inc(e.dma_start(out=y_ap[1:2, 0:128], in_=cc_sb[:]), 16)
                e.wait_ge(sG.h, sG.n)
            sC.inc(e.collective_compute(
                "AllReduce", ALU.max, replica_groups=[list(range(8))],
                ins=[cc_in.ap().opt()], outs=[cc_out.ap().opt()]))
        c_cc = sC.n

        @blk.sync
        def _(e):
            e.wait_ge(sC.h, c_cc)
            sCC.inc(e.dma_start(out=cc_sb[:], in_=cc_out.ap()), 16)
        g_cc = sCC.n

        @blk.vector
        def _(e):
            e.wait_ge(sCC.h, g_cc)
            sV.inc(e.tensor_scalar(srow[0:1, 1:2], cc_sb[0:1, 0:1],
                                   pf(1e-8), pf(1.0 / 127.0), ALU.max, ALU.mult))
            e.wait_ge(sV.h, sV.n)
            sV.inc(e.reciprocal(srow[0:1, 2:3], srow[0:1, 1:2]))
        v_h = sV.n

        @blk.tensor
        def _(e):
            e.wait_ge(sV.h, v_h)
            sT.inc(e.matmul(p8b[:], ones_rf[0:1, :],
                            srow[0:1, 2:3], start=True, stop=True))
        t_f = sT.n

        @blk.vector
        def _(e):
            e.wait_ge(sT.h, t_f)
            sV.inc(e.tensor_copy(inv_col[:], p8b[:]))
        v_inv = sV.n
    r8.close()

    if _stop("scale", [(sV.h, v_inv)],
             [(srow[0:1, 0:3], 0), (cc_sb[:], 1), (inv_col[:], 2), (amax[:], 4)]):
        return nc, tap_aps

    r13a = Region(nc)
    pK1 = [r13a.ps(f"pK1{r}", [128, 512]) for r in range(4)]
    if True:
        blk = FBLK
        t13a, s13a = sT.n, sS.n

        @blk.tensor
        def _(e):
            e.wait_ge(sG.h, g_wk)
            e.wait_ge(sS.h, s_xnT)
            for jc in range(3):
                for n4 in range(4):
                    g = 4 * jc + n4
                    if g >= 4:
                        e.wait_ge(sS.h, s13a + (g - 4) + 1)
                    for ic in range(NI):
                        e.matmul(pK1[g % 4][:, 0:512],
                                 wkb[:, 1024*ic + 128*jc:1024*ic + 128*(jc+1)],
                                 xnT[:, 2048*ic + 512*n4:2048*ic + 512*(n4+1)],
                                 start=(ic == 0), stop=False)
                    sT.inc(e.matmul(pK1[g % 4][:, 0:512],
                                    rowpb[B_BK:B_BK+1, 128*jc:128*(jc+1)],
                                    ones_rb[B_BK:B_BK+1, 0:512], start=False, stop=True))

        @blk.scalar
        def _(e):
            for g in range(12):
                jc, n4 = g // 4, g % 4
                e.wait_ge(sT.h, t13a + g + 1)
                sS.inc(e.copy(kTa[:, 2048*jc + 512*n4:2048*jc + 512*(n4+1)],
                              pK1[g % 4][:, 0:512]))
        s_kTa = sS.n
    r13a.close()

    FBLK_cm.__exit__(None, None, None)
    FBLK_cm = nc.Block()
    FBLK = FBLK_cm.__enter__()

    r9 = Region(nc)
    i32s = r9.sb("i32s", [128, 512], I32)
    qib = [r9.sb(f"qib{r}", [128, 1024], BF16) for r in range(2)]
    pT9 = [r9.ps(f"pT9{r}", [128, 512]) for r in range(2)]
    if True:
        blk = FBLK
        v9b, s9, t9 = sV.n, sS.n, sT.n

        def qib_done(m):
            return s9 + 4 * m + 4

        @blk.scalar
        def _(e):
            e.wait_ge(sV.h, v_inv)
            for m in range(NT):
                if m >= 2:
                    e.wait_ge(sT.h, t9 + 2*m - 2)
                for h in range(2):
                    sS.inc(e.activation(i32s[:], arena[m][:, 512*h:512*(h+1)],
                                        AF.Identity, scale=inv_col[:, 0:1]))
                    sS.inc(e.copy(qib[m % 2][:, 512*h:512*(h+1)], i32s[:]))
        s_casts = sS.n

        @blk.vector
        def _(e):
            for g in range(2 * NT):
                e.wait_ge(sT.h, t9 + g + 1)
                mm, aa = g // 2, g % 2
                dst = qiT[:].rearrange("p (c t) -> p c t", t=2048)[:, 4*aa:4*aa+4, 128*mm:128*(mm+1)]
                src = pT9[g % 2][:].rearrange("p (c t) -> p c t", t=128)
                sV.inc(e.tensor_copy(dst, src))
        s_qiT = sS.n
        v_qiT = sV.n

        @blk.tensor
        def _(e):
            for m in range(NT):
                e.wait_ge(sS.h, qib_done(m))
                for a in range(2):
                    g = 2 * m + a
                    if g >= 2:
                        e.wait_ge(sV.h, v9b + (g - 2) + 1)
                    for i in range(4):
                        ic = 4 * a + i
                        instr = e.matmul(pT9[g % 2][:, 128*i:128*(i+1)],
                                         qib[m % 2][:, 128*ic:128*(ic+1)],
                                         i128b[:], start=True, stop=True)
                        if i == 3:
                            sT.inc(instr)
    if _stop("qiT", [(sS.h, s_qiT)],
             [(qiT[:, 1024*k:1024*(k+1)], 128 * k) for k in range(16)]):
        return nc, tap_aps
    r9.close()
    rA.close()

    if mktap("qint", [128, 1024], BF16):
        with nc.Block() as blk:
            @blk.sync
            def _(e):
                e.wait_ge(sS.h, s_qiT)
                sD.inc(e.dma_start(out=tap_aps["qint"], in_=qib[1][:]), 16)
    if mktap("qiT", [128, 2048], BF16):
        with nc.Block() as blk:
            @blk.sync
            def _(e):
                e.wait_ge(sS.h, s_qiT)
                sD.inc(e.dma_start(out=tap_aps["qiT"], in_=qiT[:, 0:2048]), 16)

    FBLK_cm.__exit__(None, None, None)
    MBLK = st.enter_context(nc.Block())
    rKT = Region(nc)
    kT = rKT.sb("kT", [128, 5 * 2048], BF16)
    rQ = Region(nc)
    qbf = [rQ.sb(f"qbf{m}", [128, 1024], BF16) for m in range(NT)]
    rQN = Region(nc)
    QN = rQN.sb("QN", [128, 256])
    r10 = Region(nc)
    pA = [r10.ps(f"pA{r}", [128, 512]) for r in range(4)]
    if True:
        blk = MBLK
        t10, s10 = sT.n, sS.n

        @blk.tensor
        def _(e):
            e.wait_ge(sG.h, g_wq)
            e.wait_ge(sV.h, v_qiT)
            for m in range(NT):
                for n in range(2):
                    g = 2 * m + n
                    if g >= 4:
                        e.wait_ge(sS.h, s10 + (g - 4) + 1)
                    for ic in range(NI):
                        e.matmul(pA[g % 4][:, 0:512],
                                 xnT[:, 2048*ic + 128*m:2048*ic + 128*(m+1)],
                                 wqb[:, 1024*ic + 512*n:1024*ic + 512*(n+1)],
                                 start=(ic == 0), stop=False)
                    sT.inc(e.matmul(pA[g % 4][:, 0:512], ones_rb[0:1, 0:128],
                                    rowpb[B_BQ:B_BQ+1, 512*n:512*(n+1)],
                                    start=False, stop=True))

        @blk.scalar
        def _(e):
            for g in range(2 * NT):
                m, n = g // 2, g % 2
                e.wait_ge(sT.h, t10 + g + 1)
                sS.inc(e.copy(qbf[m][:, 512*n:512*(n+1)], pA[g % 4][:, 0:512]))

        @blk.vector
        def _(e):
            for m in range(NT):
                e.wait_ge(sS.h, s10 + 2 * m + 2)
                sV.inc(e.tensor_reduce(
                    QN[:, 16*m:16*(m+1)],
                    qbf[m][:].rearrange("p (h d) -> p h d", d=64),
                    AX.X, ALU.add, apply_absolute_value=True))
        v_qn = sV.n
    if _stop("q", [(sS.h, s10 + 2 * NT)],
             [(qbf[m][:], 128 * m) for m in range(NT)]):
        return nc, tap_aps
    if _stop("qn", [(sV.h, v_qn)], [(QN[:], 0)]):
        return nc, tap_aps
    r10.close()
    rWQ.close()

    if mktap("q3", [128, 1024], BF16):
        with nc.Block() as blk:
            @blk.sync
            def _(e):
                e.wait_ge(sV.h, v_qn)
                sD.inc(e.dma_start(out=tap_aps["q3"], in_=qbf[3][:]), 16)
    if mktap("qn", [128, 256]):
        with nc.Block() as blk:
            @blk.sync
            def _(e):
                e.wait_ge(sV.h, v_qn)
                sD.inc(e.dma_start(out=tap_aps["qn"], in_=QN[:]), 16)

    r13 = Region(nc)
    pK = [r13.ps(f"pK{r}", [128, 512]) for r in range(4)]
    if True:
        blk = MBLK
        t13, s13 = sT.n, sS.n

        @blk.tensor
        def _(e):
            e.wait_ge(sG.h, g_wk)
            e.wait_ge(sS.h, s10 + 2 * NT)
            for jc in range(3, NI):
                for n4 in range(4):
                    g = 4 * (jc - 3) + n4
                    if g >= 4:
                        e.wait_ge(sS.h, s13 + (g - 4) + 1)
                    for ic in range(NI):
                        e.matmul(pK[g % 4][:, 0:512],
                                 wkb[:, 1024*ic + 128*jc:1024*ic + 128*(jc+1)],
                                 xnT[:, 2048*ic + 512*n4:2048*ic + 512*(n4+1)],
                                 start=(ic == 0), stop=False)
                    sT.inc(e.matmul(pK[g % 4][:, 0:512],
                                    rowpb[B_BK:B_BK+1, 128*jc:128*(jc+1)],
                                    ones_rb[B_BK:B_BK+1, 0:512], start=False, stop=True))

        @blk.scalar
        def _(e):
            for g in range(20):
                jc, n4 = 3 + g // 4, g % 4
                e.wait_ge(sT.h, t13 + g + 1)
                sS.inc(e.copy(kT[:, 2048*(jc-3) + 512*n4:2048*(jc-3) + 512*(n4+1)],
                              pK[g % 4][:, 0:512]))
        s_kT = sS.n
    if _stop("kT", [(sS.h, s_kT)],
             [(kT[:, 1024*k:1024*(k+1)], 128 * k) for k in range(8)]):
        return nc, tap_aps
    r13.close()
    rWK.close()

    t_K = sT.n
    rVB = Region(nc)
    vbf = rVB.sb("vbf", [128, NT * 1024], BF16, side="right")
    rWV = Region(nc)
    wvb = rWV.sb("wvb", [128, NI * 1024], BF16, side="right")
    if True:
        blk = MBLK
        @blk.gpsimd
        def _(e):
            e.wait_ge(sT.h, t_K)
            for i in range(NI):
                sG.inc(e.dma_start(out=wvb[:, 1024*i:1024*(i+1)],
                                   in_=w_h["Wv"].ap()[128*i:128*(i+1), :]), 16)
        g_wv = sG.n

    r14 = Region(nc)
    pV = [r14.ps(f"pV{r}", [128, 512]) for r in range(4)]
    if True:
        blk = MBLK
        t14, s14 = sT.n, sS.n

        @blk.tensor
        def _(e):
            e.wait_ge(sG.h, g_wv)
            e.wait_ge(sS.h, s_kT)
            for m in range(NT):
                for n in range(2):
                    g = 2 * m + n
                    if g >= 4:
                        e.wait_ge(sS.h, s14 + (g - 4) + 1)
                    for ic in range(NI):
                        e.matmul(pV[g % 4][:, 0:512],
                                 xnT[:, 2048*ic + 128*m:2048*ic + 128*(m+1)],
                                 wvb[:, 1024*ic + 512*n:1024*ic + 512*(n+1)],
                                 start=(ic == 0), stop=False)
                    sT.inc(e.matmul(pV[g % 4][:, 0:512], ones_rb[B_BV:B_BV+1, 0:128],
                                    rowpb[B_BV:B_BV+1, 512*n:512*(n+1)],
                                    start=False, stop=True))

        @blk.scalar
        def _(e):
            for g in range(2 * NT):
                m, n = g // 2, g % 2
                e.wait_ge(sT.h, t14 + g + 1)
                sS.inc(e.copy(vbf[:, 1024*m + 512*n:1024*m + 512*(n+1)],
                              pV[g % 4][:, 0:512]))
        s_vbf = sS.n
    r14.close()
    rWV.close()

    r11 = Region(nc)
    QNW = r11.sb("QNW", [16, 2048])
    thr = r11.sb("thr", [16, 2048], BF16)
    thr2 = r11.sb("thr2", [16, 2048], BF16)
    u128b = r11.sb("u128b", [128, 128], BF16)
    pQNT = r11.ps("pQNT", [16, 2048])
    if True:
        blk = MBLK
        @blk.tensor
        def _(e):
            e.wait_ge(sV.h, v_qn)
            e.wait_ge(sS.h, s_vbf)
            for c in range(NT):
                sT.inc(e.matmul(pQNT[:, 128*c:128*(c+1)], QN[:, 16*c:16*(c+1)],
                                i128f[:], start=True, stop=True))
        t_qnt = sT.n

        @blk.vector
        def _(e):
            e.wait_ge(sT.h, t_qnt)
            e.tensor_scalar(u128b[:], iota_r[:], iota_c[:], None, ALU.is_ge)
            sV.inc(e.tensor_copy(QNW[:], pQNT[0:16, :]))
            for r in range(5):
                sV.inc(e.max(out=mx8[:, 8*r:8*(r+1)], in_=QNW[:]))
                if r < 4:
                    e.wait_ge(sV.h, sV.n)
                    sV.inc(e.scalar_tensor_tensor(QNW[:], QNW[:], mx8[:, 8*r+7:8*r+8],
                                                  QNW[:], ALU.is_lt, ALU.mult))
                    e.wait_ge(sV.h, sV.n)
            e.wait_ge(sV.h, sV.n)
            e.tensor_scalar(thr[:], QNW[:], mx8[:, 37:38], None, ALU.is_ge)
            sV.inc(e.tensor_scalar(thr2[:], QNW[:], pf(0.0), None, ALU.is_equal))
            e.wait_ge(sV.h, sV.n)
            sV.inc(e.tensor_tensor(thr[:], thr[:], thr2[:], ALU.max))
        v_thr = sV.n
    if mktap("mx8", [16, 40]):
        with nc.Block() as blk:
            @blk.sync
            def _(e):
                e.wait_ge(sV.h, v_thr)
                sD.inc(e.dma_start(out=tap_aps["mx8"], in_=mx8[:, 0:40]), 16)
    if mktap("mask", [16, 2048], BF16):
        with nc.Block() as blk:
            @blk.sync
            def _(e):
                e.wait_ge(sV.h, v_thr)
                sD.inc(e.dma_start(out=tap_aps["mask"], in_=thr[:]), 16)

    pMT = r11.ps("pMT", [128, 256])
    pRK = [r11.ps(f"pRK{r}", [128, 16]) for r in range(2)]
    if True:
        blk = MBLK
        @blk.tensor
        def _(e):
            e.wait_ge(sV.h, v_thr)
            for c in range(NT):
                sT.inc(e.matmul(pMT[:, 16*c:16*(c+1)], thr[:, 128*c:128*(c+1)],
                                i128b[0:16, 0:16], start=True, stop=True))
        t_mt = sT.n

        @blk.vector
        def _(e):
            e.wait_ge(sT.h, t_mt)
            sV.inc(e.tensor_copy(maskT[:], pMT[:]))
        v_mt = sV.n

        @blk.tensor
        def _(e):
            e.wait_ge(sV.h, v_mt)
            for c in range(NT):
                if c >= 2:
                    e.wait_ge(sV.h, v_mt + (c - 2) + 1)
                for c2 in range(c + 1):
                    instr = e.matmul(pRK[c % 2][:],
                                     u128b[:] if c2 == c else ones128b[:],
                                     maskT[:, 16*c2:16*(c2+1)],
                                     start=(c2 == 0), stop=(c2 == c))
                    if c2 == c:
                        sT.inc(instr)
        t_rk = sT.n - NT

        @blk.vector
        def _(e):
            for c in range(NT):
                e.wait_ge(sT.h, t_rk + c + 1)
                sV.inc(e.tensor_tensor(rank_sb[:, 16*c:16*(c+1)], pRK[c % 2][:],
                                       maskT[:, 16*c:16*(c+1)], ALU.mult))
        v_rk = sV.n
    if _stop("rank", [(sV.h, v_rk), (sV.h, v_thr)],
             [(rank_sb[:], 0), (maskT[:], 128), (mx8[:, 0:40], 256),
              (thr[:, 0:1024], 272), (thr[:, 1024:2048], 400)]):
        return nc, tap_aps
    r11.close()
    rQN.close()

    if mktap("rank", [128, 256]):
        with nc.Block() as blk:
            @blk.sync
            def _(e):
                e.wait_ge(sV.h, v_rk)
                sD.inc(e.dma_start(out=tap_aps["rank"], in_=rank_sb[:]), 16)

    r12 = Region(nc)
    PTr = [r12.sb(f"PTr{c}", [128, 1024], BF16) for c in range(8)]
    qhalf = r12.sb("qhalf", [128, NP * 128])
    qsel_s = r12.sb("qsel_s", [128, 128], BF16)
    ish64 = r12.sb("ish64", [128, 128], F32)
    pIdxL = r12.ps("pIdxL", [1, 1024])
    pIdxH = r12.ps("pIdxH", [1, 1024])
    r12a = Region(nc)
    pQS = [r12a.ps(f"pQS{r}", [128, 128]) for r in range(2)]
    pT12 = r12a.ps("pT12", [128, 128])

    if True:
        blk = MBLK
        @blk.gpsimd
        def _(e):
            e.wait_ge(sV.h, v_rk)
            e.wait_ge(sT.h, t_rk + NT)
            for c in range(8):
                instr = e.memset(PTr[c][:], 0.0)
            sW.inc(instr)
        g_pt0 = sW.n

    half_T = [0, 0]
    half_S = [0, 0]
    half_V = [0, 0]
    for half in range(2):
        cbase = 8 * half
        if True:
            blk = MBLK
            v12, t12, s12 = sV.n, sT.n, sS.n

            @blk.vector
            def _(e):
                e.wait_ge(sW.h, g_pt0)
                e.wait_ge(sV.h, v_rk)
                if half == 1:
                    e.wait_ge(sT.h, half_T[0])
                    e.wait_ge(sS.h, half_S[0])
                for cc in range(8):
                    c = cbase + cc
                    for s in range(2):
                        dst = PTr[cc][:].rearrange("p (j r) -> p j r", r=128)[:, :, 38*s:38*(s+1)]
                        src = rank_sb[:, 16*c + s:16*c + 16:2].unsqueeze(-1).to_broadcast([128, 8, 38])
                        io = iota38[:].unsqueeze(1).to_broadcast([128, 8, 38])
                        instr = e.tensor_tensor(dst, src, io, ALU.is_equal)
                        if s == 1:
                            sV.inc(instr)
                if half == 1:
                    for j in range(NP):
                        e.wait_ge(sT.h, (t12 + 9) if j == 0 else (t12 + 8 + 2 * j))
                        if j >= 1:
                            e.wait_ge(sT.h, t12 + 2 * j + 9)
                        sV.inc(e.scalar_tensor_tensor(
                            qsel_s[0:76, 0:128], pQS[j % 2][0:76, :], pf(0.125),
                            qhalf[0:76, 128*j:128*(j+1)], ALU.mult, ALU.add))

            @blk.tensor
            def _(e):
                for cc in range(8):
                    c = cbase + cc
                    e.wait_ge(sV.h, v12 + cc + 1)
                    for n in range(2):
                        e.matmul(pIdxL[0:1, 512*n:512*(n+1)],
                                 iotaC2[:, 2*c:2*c+1],
                                 PTr[cc][:, 512*n:512*(n+1)],
                                 start=(c == 0), stop=(c == NT - 1))
                        instr = e.matmul(pIdxH[0:1, 512*n:512*(n+1)],
                                         iotaC2[:, 2*c+1:2*c+2],
                                         PTr[cc][:, 512*n:512*(n+1)],
                                         start=(c == 0), stop=(c == NT - 1))
                    sT.inc(instr)
                def gather(j):
                    if j >= 2:
                        if half == 0:
                            e.wait_ge(sS.h, s12 + (j - 2) + 1)
                        else:
                            e.wait_ge(sV.h, v12 + 8 + (j - 2) + 1)
                    for cc in range(8):
                        c = cbase + cc
                        instr = e.matmul(pQS[j % 2][0:76, :],
                                         PTr[cc][:, 128*j:128*j + 76],
                                         qbf[c][:, 128*j:128*(j+1)],
                                         start=(cc == 0), stop=(cc == 7))
                    sT.inc(instr)
                def transpose(j):
                    e.wait_ge(sV.h, v12 + 8 + j + 1)
                    if j >= 1:
                        e.wait_ge(sS.h, s12 + j)
                    sT.inc(e.matmul(pT12[:, 0:76], qsel_s[0:76, 0:128],
                                    i128b[0:76, 0:76], start=True, stop=True))
                if half == 0:
                    for j in range(NP):
                        gather(j)
                else:
                    for j in range(NP):
                        gather(j)
                        if j >= 1:
                            transpose(j - 1)
                    transpose(NP - 1)
            half_T[half] = sT.n

            if half == 0:
                @blk.scalar
                def _(e):
                    for j in range(NP):
                        e.wait_ge(sT.h, t12 + 8 + j + 1)
                        sS.inc(e.activation(qhalf[0:76, 128*j:128*(j+1)],
                                            pQS[j % 2][0:76, :], AF.Identity,
                                            scale=f32(0.125)))
            else:
                @blk.scalar
                def _(e):
                    for j in range(NP):
                        e.wait_ge(sT.h, (t12 + 2 * j + 11) if j < NP - 1 else (t12 + 24))
                        e.copy(qselT[j][0:64, 0:38], pT12[0:64, 0:38])
                        sS.inc(e.copy(qselT[j][64:128, 38:76], pT12[64:128, 38:76]))
            half_S[half] = sS.n
            half_V[half] = sV.n
    s_gather = sS.n

    r12a.close()
    pIB = [r12.ps(f"pIB{r}", [128, 512]) for r in range(2)]
    if True:
        blk = MBLK
        @blk.vector
        def _(e):
            e.wait_ge(sT.h, half_T[1])
            e.tensor_copy(fscr[0][0:1, :], pIdxL[0:1, :])
            sV.inc(e.tensor_copy(fscr[1][0:1, :], pIdxH[0:1, :]))
            e.wait_ge(sV.h, sV.n)
            sV.inc(e.scalar_tensor_tensor(rowpk[R_IDX:R_IDX+1, :], fscr[1][0:1, :],
                                          pf(128.0), fscr[0][0:1, :], ALU.mult, ALU.add))
        v_idx = sV.n

        @blk.tensor
        def _(e):
            e.wait_ge(sV.h, v_idx)
            for n in range(2):
                sT.inc(e.matmul(pIB[n][:, 0:512], ones_rf[R_IDX:R_IDX+1, :],
                                rowpk[R_IDX:R_IDX+1, 512*n:512*(n+1)],
                                start=True, stop=True))
        t_ib = sT.n

        @blk.vector
        def _(e):
            e.wait_ge(sT.h, t_ib)
            e.tensor_copy(idx_bc[:, 0:512], pIB[0][:, 0:512])
            sV.inc(e.tensor_copy(idx_bc[:, 512:1024], pIB[1][:, 0:512]))
            sV.inc(e.tensor_scalar(ish64[:], iota_r[:], pf(64.0), iota_c[:],
                                   ALU.add, ALU.is_equal))
            e.wait_ge(sV.h, sV.n)
            for j in range(NP):
                sV.inc(e.tensor_tensor(fscr[0][0:38, 0:38], idx_bc[0:38, 128*j:128*j+38],
                                       i128f[0:38, 0:38], ALU.mult))
                e.wait_ge(sV.h, sV.n)
                sV.inc(e.tensor_reduce(idxcol[0:38, j:j+1], fscr[0][0:38, 0:38], AX.X, ALU.add))
                sV.inc(e.tensor_tensor(fscr[0][64:102, 0:38], idx_bc[64:102, 128*j+38:128*j+76],
                                       ish64[64:102, 0:38], ALU.mult))
                e.wait_ge(sV.h, sV.n)
                sV.inc(e.tensor_reduce(idxcol[64:102, j:j+1], fscr[0][64:102, 0:38],
                                       AX.X, ALU.add))
        v_idxc = sV.n
    if _stop("sel", [(sS.h, s_gather), (sV.h, v_idxc)],
             [(idx_bc[:], 0), (idxcol[:], 128)] +
             [(qselT[j][:], 256 + 128 * j) for j in range(NP)]):
        return nc, tap_aps
    r12.close()
    rQ.close()

    if mktap("idxbc", [128, 1024]):
        with nc.Block() as blk:
            @blk.sync
            def _(e):
                e.wait_ge(sV.h, v_idxc)
                sD.inc(e.dma_start(out=tap_aps["idxbc"], in_=idx_bc[:]), 16)
    if mktap("idxcol", [128, 8]):
        with nc.Block() as blk:
            @blk.sync
            def _(e):
                e.wait_ge(sV.h, v_idxc)
                sD.inc(e.dma_start(out=tap_aps["idxcol"], in_=idxcol[:]), 16)
    if mktap("qselT0", [128, 76], BF16):
        with nc.Block() as blk:
            @blk.sync
            def _(e):
                e.wait_ge(sS.h, s_gather)
                sD.inc(e.dma_start(out=tap_aps["qselT0"], in_=qselT[0][:]), 16)

    rAT = Region(nc)
    attn = [rAT.sb(f"attn{c}", [128, 1024], BF16, side="right") for c in range(NT)]
    maskC = [rAT.sb(f"maskC{r}", [128, 1024], BF16, side="right") for r in range(2)]
    rPS2 = Region(nc)
    pRS = rPS2.ps("pRS", [1, 1024])
    rPS1 = Region(nc)
    pS15 = [rPS1.ps(f"pS15{r}", [128, 1024]) for r in range(2)]
    if True:
        blk = MBLK
        t15, s15, v15 = sT.n, sS.n, sV.n

        @blk.vector
        def _(e):
            e.wait_ge(sV.h, v_idxc)
            for r in range(2):
                ph = pS15[r][:].rearrange("p (j w) -> p j w", w=128)[:, :, 76:128]
                instr = e.memset(ph, 0.0)
            sV.inc(instr)
            for c in range(NT):
                sV.inc(e.tensor_scalar(maskC[c % 2][:], idx_bc[:],
                                       iota2d[:, c:c+1], None, ALU.is_ge))
                e.wait_ge(sV.h, sV.n)
                e.wait_ge(sS.h, s15 + c + 1)
                sV.inc(e.tensor_tensor(attn[c][:], attn[c][:], maskC[c % 2][:], ALU.mult))

        @blk.tensor
        def _(e):
            e.wait_ge(sS.h, s_kT)
            e.wait_ge(sS.h, s_gather)
            e.wait_ge(sV.h, v15 + 1)
            for c in range(NT):
                if c >= 2:
                    e.wait_ge(sS.h, s15 + (c - 2) + 1)
                for j in range(NP):
                    ksrc, jb = (kTa, j) if j < 3 else (kT, j - 3)
                    instr = e.matmul(pS15[c % 2][:, 128*j:128*j + 76],
                                     ksrc[:, 2048*jb + 128*c:2048*jb + 128*(c+1)],
                                     qselT[j][:, 0:76], start=True, stop=True)
                sT.inc(instr)
                e.wait_ge(sV.h, v15 + 1 + 2 * c + 2)
                for n in range(2):
                    instr = e.matmul(pRS[0:1, 512*n:512*(n+1)], ones_cb[:],
                                     attn[c][:, 512*n:512*(n+1)],
                                     start=(c == 0), stop=(c == NT - 1))
                sT.inc(instr)

        @blk.scalar
        def _(e):
            for c in range(NT):
                e.wait_ge(sT.h, t15 + 2 * c + 1)
                sS.inc(e.activation(attn[c][:], pS15[c % 2][:], AF.Exp))
        s_attn = sS.n
        t_attn = sT.n
        v_attn = sV.n
    if _stop("attn", [(sS.h, s_attn), (sV.h, v_attn), (sT.h, t_attn)],
             [(attn[c][:], 128 * c) for c in range(NT)]):
        return nc, tap_aps
    rPS1.close()

    if mktap("attn5", [128, 1024], BF16):
        with nc.Block() as blk:
            @blk.sync
            def _(e):
                e.wait_ge(sV.h, v_attn)
                sD.inc(e.dma_start(out=tap_aps["attn5"], in_=attn[5][:]), 16)

    rAV = Region(nc)
    recipb = rAV.sb("recipb", [128, 1024], side="right")
    pAV = [rAV.ps(f"pAV{r}", [128, 128]) for r in range(2)]
    pRB = [rAV.ps(f"pRB{r}", [128, 512]) for r in range(2)]
    if True:
        blk = MBLK
        @blk.vector
        def _(e):
            e.wait_ge(sT.h, t_attn)
            sV.inc(e.reciprocal(rowpk[R_RECIP:R_RECIP+1, :], pRS[0:1, :]))
        v_rc = sV.n

        @blk.tensor
        def _(e):
            e.wait_ge(sV.h, v_rc)
            e.wait_ge(sS.h, s_attn)
            for n in range(2):
                sT.inc(e.matmul(pRB[n][:, 0:512], ones_rf[0:1, :],
                                rowpk[R_RECIP:R_RECIP+1, 512*n:512*(n+1)],
                                start=True, stop=True))
        t_rb = sT.n

        @blk.vector
        def _(e):
            e.wait_ge(sT.h, t_rb)
            e.tensor_copy(recipb[:, 0:512], pRB[0][:, 0:512])
            sV.inc(e.tensor_copy(recipb[:, 512:1024], pRB[1][:, 0:512]))
            e.wait_ge(sV.h, sV.n)
        v_rb = sV.n

        @blk.tensor
        def _(e):
            e.wait_ge(sS.h, s_vbf)
            for j in range(NP):
                if j >= 2:
                    e.wait_ge(sV.h, v_rb + (j - 2) + 1)
                for c in range(NT):
                    instr = e.matmul(pAV[j % 2][:, 0:76],
                                     vbf[:, 1024*c + 128*j:1024*c + 128*(j+1)],
                                     attn[c][:, 128*j:128*j + 76],
                                     start=(c == 0), stop=(c == NT - 1))
                sT.inc(instr)
        t_av = sT.n - NP

        @blk.vector
        def _(e):
            for j in range(NP):
                e.wait_ge(sT.h, t_av + j + 1)
                e.tensor_tensor(soT[j][0:64, 0:38], pAV[j % 2][0:64, 0:38],
                                recipb[0:64, 128*j:128*j+38], ALU.mult)
                sV.inc(e.tensor_tensor(soT[j][64:128, 38:76], pAV[j % 2][64:128, 38:76],
                                       recipb[64:128, 128*j+38:128*j+76], ALU.mult))
        v_soT = sV.n
    rAV.close()
    rPS2.close()
    rAT.close()
    rKT.close()

    rIO = Region(nc)
    iota1k = rIO.sb("iota1k", [128, 1024])
    rWO = Region(nc)
    wob = rWO.sb("wob", [128, NI * 1024], BF16, side="right")
    if True:
        blk = MBLK
        @blk.gpsimd
        def _(e):
            e.wait_ge(sV.h, v_soT)
            kw = dict(allow_small_or_imprecise_dtypes=True)
            sW.inc(e.iota(iota1k[:], [[1, 1024]], base=0, channel_multiplier=0, **kw))
            for i in range(NI):
                sG.inc(e.dma_start(out=wob[:, 1024*i:1024*(i+1)],
                                   in_=w_h["Wo"].ap()[128*i:128*(i+1), :]), 16)
        g_wo = sG.n
        g_io = sW.n

    if _stop("soT", [(sV.h, v_soT)],
             [(soT[j][:], 128 * j) for j in range(NP)] + [(recipb[:], 1024)]):
        return nc, tap_aps

    if mktap("soT0", [128, 76], BF16):
        with nc.Block() as blk:
            @blk.sync
            def _(e):
                e.wait_ge(sV.h, v_soT)
                sD.inc(e.dma_start(out=tap_aps["soT0"], in_=soT[0][:]), 16)

    rBO = Region(nc)
    bo_bc = rBO.sb("bo_bc", [128, 1024])
    rBOp = Region(nc)
    pBO = [rBOp.ps(f"pBO{r}", [128, 512]) for r in range(2)]
    if True:
        blk = MBLK
        @blk.tensor
        def _(e):
            e.wait_ge(sV.h, v_soT)
            for n in range(2):
                sT.inc(e.matmul(pBO[n][:, 0:512], ones_rf[R_BO:R_BO+1, :],
                                rowpk[R_BO:R_BO+1, 512*n:512*(n+1)],
                                start=True, stop=True))
        t_bo = sT.n

        @blk.vector
        def _(e):
            e.wait_ge(sT.h, t_bo)
            e.tensor_copy(bo_bc[:, 0:512], pBO[0][:, 0:512])
            sV.inc(e.tensor_copy(bo_bc[:, 512:1024], pBO[1][:, 0:512]))
        v_bo = sV.n
    rBOp.close()

    rGH = Region(nc)
    G = [rGH.sb(f"G{j}", [128, 1024], BF16) for j in range(NP)]
    r17 = Region(nc)
    pG = [r17.ps(f"pG{r}", [128, 1024]) for r in range(2)]
    if True:
        blk = MBLK
        t17, s17 = sT.n, sS.n

        @blk.gpsimd
        def _(e):
            e.wait_ge(sV.h, v_soT)
            for j in range(NP):
                instr = e.memset(G[j][:], 0.0)
            sW.inc(instr)
        g_gz = sW.n

        @blk.tensor
        def _(e):
            e.wait_ge(sG.h, g_wo)
            e.wait_ge(sV.h, v_bo)
            for j in range(NP):
                e.wait_ge(sV.h, v_soT - NP + j + 1)
                if j >= 2:
                    e.wait_ge(sS.h, s17 + (j - 2) + 1)
                for s in range(2):
                    for n in range(2):
                        instr = e.matmul(
                            pG[j % 2][64*s:64*s + 38, 512*n:512*(n+1)],
                            soT[j][64*s:64*s + 64, 38*s:38*s + 38],
                            wob[64*s:64*s + 64, 1024*j + 512*n:1024*j + 512*(n+1)],
                            start=True, stop=True)
                sT.inc(instr)

        @blk.scalar
        def _(e):
            e.wait_ge(sW.h, g_gz)
            for j in range(NP):
                e.wait_ge(sT.h, t17 + j + 1)
                e.copy(G[j][0:38, :], pG[j % 2][0:38, :])
                sS.inc(e.copy(G[j][64:102, :], pG[j % 2][64:102, :]))
        s_G = sS.n
    r17.close()
    rWO.close()
    rVB.close()
    rKTa.close()
    rXT.close()

    if _stop("G", [(sS.h, s_G)], [(G[j][:], 128 * j) for j in range(NP)]):
        return nc, tap_aps

    if mktap("G0", [128, 1024], BF16):
        with nc.Block() as blk:
            @blk.sync
            def _(e):
                e.wait_ge(sS.h, s_G)
                sD.inc(e.dma_start(out=tap_aps["G0"], in_=G[0][:]), 16)

    rP = Region(nc)
    P = [rP.sb(f"P{k}", [128, 2048], BF16) for k in range(NP)]
    rSG = Region(nc)
    scG = [rSG.sb(f"scG{m}", [128, 1024], BF16) for m in range(NT)]
    rSGp = Region(nc)
    pSCG = [rSGp.ps(f"pSCG{r}", [128, 1024]) for r in range(2)]
    s_scg = {}
    if True:
        blk = MBLK
        @blk.vector
        def _(e):
            e.wait_ge(sW.h, g_io)
            e.wait_ge(sV.h, v_idxc)
            for h in range(2):
                for k in range(NP):
                    instr = e.tensor_scalar(P[k][:, 1024*h:1024*(h+1)],
                                            iota1k[:], pf(1024.0 * h),
                                            idxcol[:, k:k+1], ALU.add, ALU.is_equal)
                sV.inc(instr)
        v_P0 = sV.n - 2

    if True:
        blk = MBLK
        t_pre = sT.n

        def tsg_pre(m):
            return t_pre + 2 * (m + 1)

        @blk.scalar
        def _(e):
            for m in range(6):
                e.wait_ge(sT.h, tsg_pre(m))
                sS.inc(e.copy(scG[m][:], pSCG[m % 2][:]))
                s_scg[m] = sS.n

        @blk.tensor
        def _(e):
            e.wait_ge(sS.h, s_G)
            e.wait_ge(sV.h, v_P0 + 1)
            for m in range(6):
                if m >= 2:
                    e.wait_ge(sS.h, s_scg[m - 2])
                for n in range(2):
                    for k in range(NP):
                        instr = e.matmul(pSCG[m % 2][:, 512*n:512*(n+1)],
                                         P[k][:, 128*m:128*(m+1)],
                                         G[k][:, 512*n:512*(n+1)],
                                         start=(k == 0), stop=(k == NP - 1))
                    sT.inc(instr)

    rSN = Region(nc)
    signT = rSN.sb("signT", [128, NI * 1024], BF16, side="right")
    mkb = rSN.sb("mkb", [128, 1024], BF16, side="right")
    sgn = [rSN.sb(f"sgn{r}", [128, 1024], BF16, side="right") for r in range(2)]
    wfr = [rSN.sb(f"wfr{r}", [128, 1024], side="right") for r in range(2)]
    rSNp = Region(nc)
    p18 = rSNp.ps("p18", [1, 8])
    p18c = rSNp.ps("p18c", [128, 1])
    pT18 = [rSNp.ps(f"pT18{r}", [128, 512]) for r in range(2)]

    wf1 = [0] * NI
    if True:
        blk = MBLK
        v18, t18 = sV.n, sT.n

        @blk.sync
        def _(e):
            e.wait_ge(sS.h, s_G)
            for rr in range(NI):
                if rr >= 2:
                    e.wait_ge(sV.h, v18 + (rr - 2) + 1)
                sF[rr % 2].inc(e.dma_start(out=wfr[rr % 2][:],
                                   in_=w_h["Wf"].ap()[128*rr:128*(rr+1), :]), 16)
                wf1[rr] = sF[rr % 2].n

        @blk.vector
        def _(e):
            for rr in range(NI):
                e.wait_ge(sF[rr % 2].h, wf1[rr])
                sV.inc(e.tensor_reduce(wstat[:, rr:rr+1], wfr[rr % 2][:], AX.X,
                                       ALU.add, apply_absolute_value=True))
            e.wait_ge(sV.h, sV.n)
            sV.inc(e.tensor_reduce(SL(S_TMP)[:, 2:3], wstat[:, 0:8], AX.X, ALU.add))
        v_wsum = sV.n

        @blk.tensor
        def _(e):
            e.wait_ge(sV.h, v_wsum)
            sT.inc(e.matmul(p18[0:1, 0:1], SL(S_TMP)[:, 2:3], ones_cf[:], start=True, stop=True))
        t_wtot = sT.n

        @blk.vector
        def _(e):
            e.wait_ge(sT.h, t_wtot)
            sV.inc(e.tensor_scalar(srow[0:1, 3:4], p18[0:1, 0:1],
                                   pf(1.0 / (1024.0 * 1024.0)), pf(0.7),
                                   ALU.mult, ALU.mult))
        v_dl = sV.n

        @blk.tensor
        def _(e):
            e.wait_ge(sV.h, v_dl)
            sT.inc(e.matmul(p18c[:], ones_rf[0:1, :],
                            srow[0:1, 3:4], start=True, stop=True))
        t_dl = sT.n

        @blk.vector
        def _(e):
            e.wait_ge(sT.h, t_dl)
            sV.inc(e.tensor_copy(dcol[:], p18c[:]))
        v_dcol = sV.n

    wf2 = [0] * NI
    if True:
        blk = MBLK
        v18b, s18b, t18b = sV.n, sS.n, sT.n

        tsg, t18d = {}, {}
        _tc = 0
        for _m in range(6, 16):
            _tc += 2
            tsg[_m] = t18b + _tc
            if _m <= 13:
                _rr = _m - 6
                _tc += 1
                t18d[2 * _rr] = t18b + _tc
                _tc += 1
                t18d[2 * _rr + 1] = t18b + _tc

        @blk.sync
        def _(e):
            for rr in range(NI):
                if rr >= 2:
                    e.wait_ge(sV.h, v18b + 3 * (rr - 2) + 3)
                else:
                    e.wait_ge(sV.h, v18 + NI)
                sX[rr % 2].inc(e.dma_start(out=wfr[rr % 2][:],
                                   in_=w_h["Wf"].ap()[128*rr:128*(rr+1), :]), 16)
                wf2[rr] = sX[rr % 2].n

        s_abs_d, s_sign_d, s_evac18 = {}, {}, {}

        def evac_sg(e, g):
            rr, a = g // 2, g % 2
            e.wait_ge(sT.h, t18d[g])
            dst = signT[:].rearrange("p (c o) -> p c o", o=1024)[:, 4*a:4*a+4, 128*rr:128*(rr+1)]
            src = pT18[g % 2][:].rearrange("p (c o) -> p c o", o=128)
            sS.inc(e.copy(dst, src))
            s_evac18[g] = sS.n

        @blk.scalar
        def _(e):
            for rr in range(NI):
                e.wait_ge(sX[rr % 2].h, wf2[rr])
                if rr == 0:
                    e.wait_ge(sV.h, v_dcol)
                if rr >= 1:
                    e.wait_ge(sV.h, v18b + 3 * rr)
                sS.inc(e.activation(fscr[0][:], wfr[rr % 2][:], AF.Abs))
                s_abs_d[rr] = sS.n
                e.wait_ge(sV.h, v18b + 3 * rr + 2)
                sS.inc(e.activation(fscr[0][:], wfr[rr % 2][:], AF.Sign))
                s_sign_d[rr] = sS.n
                if rr >= 1:
                    for a in range(2):
                        evac_sg(e, 2 * (rr - 1) + a)
                m = 6 + rr
                e.wait_ge(sT.h, tsg[m])
                sS.inc(e.copy(scG[m][:], pSCG[m % 2][:]))
                s_scg[m] = sS.n
            for a in range(2):
                evac_sg(e, 2 * (NI - 1) + a)
            for m in (14, 15):
                e.wait_ge(sT.h, tsg[m])
                sS.inc(e.copy(scG[m][:], pSCG[m % 2][:]))
                s_scg[m] = sS.n
        s_signT = s_evac18[2 * NI - 1]
        s_scg_all = s_scg[NT - 1]

        @blk.vector
        def _(e):
            e.wait_ge(sV.h, v_dcol)
            for rr in range(NI):
                e.wait_ge(sS.h, s_abs_d[rr])
                sV.inc(e.scalar_tensor_tensor(fscr[1][:], fscr[0][:], dcol[:],
                                              fscr[0][:], ALU.is_gt, ALU.mult,
                                              accum_out=wstat[:, 8+rr:9+rr]))
                sV.inc(e.tensor_scalar(mkb[:], fscr[0][:], dcol[:], None, ALU.is_gt,
                                       ALU.add, accum_out=wstat[:, 16+rr:17+rr]))
                e.wait_ge(sS.h, s_sign_d[rr])
                if rr >= 2:
                    e.wait_ge(sT.h, t18d[2 * (rr - 2) + 1])
                sV.inc(e.tensor_tensor(sgn[rr % 2][:], fscr[0][:], mkb[:], ALU.mult))

        @blk.tensor
        def _(e):
            e.wait_ge(sV.h, v_P0 + 2)
            for m in range(6, 16):
                e.wait_ge(sS.h, s_scg[m - 2])
                for n in range(2):
                    for k in range(NP):
                        instr = e.matmul(pSCG[m % 2][:, 512*n:512*(n+1)],
                                         P[k][:, 128*m:128*(m+1)],
                                         G[k][:, 512*n:512*(n+1)],
                                         start=(k == 0), stop=(k == NP - 1))
                    sT.inc(instr)
                if m <= 13:
                    rr = m - 6
                    e.wait_ge(sV.h, v18b + 3 * rr + 3)
                    for a in range(2):
                        g = 2 * rr + a
                        if g >= 2:
                            e.wait_ge(sS.h, s_evac18[g - 2])
                        for i in range(4):
                            ic = 4 * a + i
                            instr = e.matmul(pT18[g % 2][:, 128*i:128*(i+1)],
                                             sgn[rr % 2][:, 128*ic:128*(ic+1)],
                                             i128b[:], start=True, stop=True)
                            if i == 3:
                                sT.inc(instr)

    if True:
        blk = MBLK
        v18c = sV.n

        @blk.vector
        def _(e):
            e.tensor_copy(fscr[0][:, 2:3], fscr[1][:, 0:1])
            sV.inc(e.tensor_copy(fscr[0][:, 3:4], fscr[1][:, 1:2]))
            e.wait_ge(sV.h, sV.n)
            sV.inc(e.tensor_reduce(SL(S_TMP)[:, 0:1], wstat[:, 8:16], AX.X, ALU.add))
            sV.inc(e.tensor_reduce(SL(S_TMP)[:, 1:2], wstat[:, 16:24], AX.X, ALU.add))
            e.wait_ge(sV.h, sV.n)
        v_al = sV.n

        @blk.tensor
        def _(e):
            e.wait_ge(sV.h, v_al)
            e.matmul(p18[0:1, 1:2], SL(S_TMP)[:, 0:1], ones_cf[:], start=True, stop=True)
            sT.inc(e.matmul(p18[0:1, 2:3], SL(S_TMP)[:, 1:2], ones_cf[:], start=True, stop=True))
        t_al = sT.n

        @blk.vector
        def _(e):
            e.wait_ge(sT.h, t_al)
            sV.inc(e.tensor_scalar(srow[0:1, 4:5], p18[0:1, 2:3], pf(1.0), None, ALU.max))
            e.wait_ge(sV.h, sV.n)
            sV.inc(e.reciprocal(srow[0:1, 5:6], srow[0:1, 4:5]))
            sV.inc(e.tensor_copy(fscr[0][0:1, 0:1], p18[0:1, 1:2]))
            e.wait_ge(sV.h, sV.n)
            sV.inc(e.tensor_tensor(srow[0:1, 6:7], fscr[0][0:1, 0:1],
                                   srow[0:1, 5:6], ALU.mult))
            e.wait_ge(sV.h, sV.n)
            sV.inc(e.tensor_tensor(srow[0:1, 7:8], srow[0:1, 6:7],
                                   srow[0:1, 1:2], ALU.mult))
        v_sa = sV.n

        @blk.tensor
        def _(e):
            e.wait_ge(sV.h, v_sa)
            sT.inc(e.matmul(p18c[:], ones_rf[0:1, :],
                            srow[0:1, 7:8], start=True, stop=True))
        t_sa = sT.n

        @blk.vector
        def _(e):
            e.wait_ge(sT.h, t_sa)
            sV.inc(e.tensor_copy(sa_col[:], p18c[:]))
        v_sacol = sV.n

    if _stop("sign", [(sS.h, s_signT), (sV.h, v_sacol)],
             [(signT[:, 1024*k:1024*(k+1)], 128 * k) for k in range(NI)] +
             [(srow[:], 1536), (dcol[:], 1537), (sa_col[:], 1538)]):
        return nc, tap_aps

    rSNp.close()
    rSGp.close()
    rFN = Region(nc)
    ffnr = [rFN.sb(f"ffnr{r}", [128, 1024]) for r in range(3)]
    r21 = Region(nc)
    x2r = [r21.sb(f"x2r{r}", [128, 1024]) for r in range(2)]
    outr = [r21.sb(f"outr{r}", [128, 1024]) for r in range(2)]
    r19 = Region(nc)
    pF = [r19.ps(f"pF{r}", [128, 512]) for r in range(4)]
    x2d = [0] * NT
    outd = [0] * NT
    if True:
        blk = MBLK
        t19, s19, v21, w21 = sT.n, sS.n, sV.n, sW.n

        @blk.sync
        def _(e):
            e.wait_ge(sS.h, s_G)
            for m in range(NT):
                if m >= 2:
                    e.wait_ge(sV.h, v21 + 2 * (m - 2) + 2)
                sX[m % 2].inc(e.dma_start(out=x2r[m % 2][:], in_=x_ap[128*m:128*(m+1), :]), 16)
                x2d[m] = sX[m % 2].n
                if m >= 2:
                    e.wait_ge(sV.h, v21 + 2 * (m - 2) + 2)
                    sO[m % 2].inc(e.dma_start(out=y_ap[128*(m-2):128*(m-1), :],
                                       in_=outr[(m - 2) % 2][:]), 16)
                    outd[m - 2] = sO[m % 2].n
            for m in (NT - 2, NT - 1):
                e.wait_ge(sV.h, v21 + 2 * m + 2)
                sO[m % 2].inc(e.dma_start(out=y_ap[128*m:128*(m+1), :], in_=outr[m % 2][:]), 16)
                outd[m] = sO[m % 2].n
            e.wait_ge(sO[0].h, sO[0].n)
            e.wait_ge(sO[1].h, sO[1].n)
            e.wait_ge(sD.h, sD.n)

        @blk.gpsimd
        def _(e):
            e.wait_ge(sV.h, v_bo)
            for m in range(NT):
                e.wait_ge(sX[m % 2].h, x2d[m])
                sW.inc(e.tensor_tensor(x2r[m % 2][:], x2r[m % 2][:], bo_bc[:], ALU.add))

        @blk.tensor
        def _(e):
            e.wait_ge(sS.h, s_signT)
            e.wait_ge(sV.h, v_sacol)
            e.wait_ge(sV.h, v_qiT)
            e.wait_ge(sS.h, s_scg_all)
            for m in range(NT):
                for n in range(2):
                    g = 2 * m + n
                    if g >= 4:
                        e.wait_ge(sS.h, s19 + (g - 4) + 1)
                    for ic in range(NI):
                        instr = e.matmul(pF[g % 4][:, 0:512],
                                         qiT[:, 2048*ic + 128*m:2048*ic + 128*(m+1)],
                                         signT[:, 1024*ic + 512*n:1024*ic + 512*(n+1)],
                                         start=(ic == 0), stop=(ic == NI - 1))
                    sT.inc(instr)

        @blk.scalar
        def _(e):
            for g in range(2 * NT):
                m, n = g // 2, g % 2
                e.wait_ge(sT.h, t19 + g + 1)
                if m >= 3 and n == 0:
                    e.wait_ge(sV.h, v21 + 2 * (m - 3) + 2)
                sS.inc(e.copy(ffnr[m % 3][:, 512*n:512*(n+1)], pF[g % 4][:, 0:512]))

        @blk.vector
        def _(e):
            for m in range(NT):
                e.wait_ge(sS.h, s19 + 2 * m + 2)
                e.wait_ge(sW.h, w21 + m + 1)
                if m >= 2:
                    e.wait_ge(sO[m % 2].h, outd[m - 2] if outd[m - 2] else 0)
                sV.inc(e.scalar_tensor_tensor(fscr[1][:], ffnr[m % 3][:], sa_col[:],
                                              x2r[m % 2][:], ALU.mult, ALU.add))
                e.wait_ge(sV.h, sV.n)
                sV.inc(e.tensor_tensor(outr[m % 2][:], fscr[1][:], scG[m][:], ALU.add))

    r19.close()
    r21.close()
    rFN.close()
    rSN.close()
    rSG.close()
    rP.close()
    rGH.close()
    rBO.close()
    rIO.close()
    st.close()
    return nc, tap_aps


_IN_ORDER = ["x", "Wq", "bq", "Wk", "bk", "Wv", "bv", "Wo", "bo", "Wf", "gamma", "beta"]


def kernel(L0_attn_x, L0_attn_Wq, L0_attn_bq, L0_attn_Wk, L0_attn_bk,
           L0_attn_Wv, L0_attn_bv, L0_attn_Wo, L0_attn_bo,
           L0_ffn_weight, L0_ffn_gamma, L0_ffn_beta):
    x = np.ascontiguousarray(np.asarray(L0_attn_x, dtype=np.float32))
    shared = {
        "Wq": np.ascontiguousarray(np.asarray(L0_attn_Wq, dtype=np.float32)),
        "bq": np.ascontiguousarray(np.asarray(L0_attn_bq, dtype=np.float32)),
        "Wk": np.ascontiguousarray(np.asarray(L0_attn_Wk, dtype=np.float32)),
        "bk": np.ascontiguousarray(np.asarray(L0_attn_bk, dtype=np.float32)),
        "Wv": np.ascontiguousarray(np.asarray(L0_attn_Wv, dtype=np.float32)),
        "bv": np.ascontiguousarray(np.asarray(L0_attn_bv, dtype=np.float32)),
        "Wo": np.ascontiguousarray(np.asarray(L0_attn_Wo, dtype=np.float32)),
        "bo": np.ascontiguousarray(np.asarray(L0_attn_bo, dtype=np.float32)),
        "Wf": np.ascontiguousarray(np.asarray(L0_ffn_weight, dtype=np.float32)),
        "gamma": np.ascontiguousarray(np.asarray(L0_ffn_gamma, dtype=np.float32)),
        "beta": np.ascontiguousarray(np.asarray(L0_ffn_beta, dtype=np.float32)),
    }
    nc, _ = build_nc()
    in_maps = [dict(shared, x=np.ascontiguousarray(x[b])) for b in range(8)]
    res = run_bass_kernel_spmd(nc, in_maps, core_ids=list(range(8)))
    return np.stack([np.asarray(res.results[b]["y"]) for b in range(8)], axis=0)


# version 22 (final)
# speedup vs baseline: 1.0687x; 1.0687x over previous
import numpy as np
from contextlib import ExitStack

import concourse.bass as bass
import concourse.mybir as mybir
from concourse.bass_utils import run_bass_kernel_spmd

F32 = mybir.dt.float32
BF16 = mybir.dt.bfloat16
I32 = mybir.dt.int32
AF = mybir.ActivationFunctionType
ALU = mybir.AluOpType
AX = mybir.AxisListType

T, D, H, HD, U = 2048, 1024, 16, 64, 38
NT, NI, NP = T // 128, D // 128, H // 2
f32 = np.float32
pf = float


class Ctr:
    def __init__(self, h):
        self.h, self.n = h, 0

    def inc(self, instr, k=1):
        instr.then_inc(self.h, k)
        self.n += k
        return instr


class Region:
    def __init__(self, nc):
        self.nc, self.cms = nc, []

    def sb(self, name, shp, dt=F32, side=None):
        cm = self.nc.sbuf_tensor(name, shp, dt, side=side) if side else self.nc.sbuf_tensor(name, shp, dt)
        h = cm.__enter__()
        self.cms.append(cm)
        return h

    def ps(self, name, shp):
        cm = self.nc.psum_tensor(name, shp, F32)
        h = cm.__enter__()
        self.cms.append(cm)
        return h

    def close(self):
        for cm in reversed(self.cms):
            cm.__exit__(None, None, None)
        self.cms = []


def build_nc(taps=(), stop_after=None):
    nc = bass.Bass("TRN2", target_bir_lowering=False,
                   detect_race_conditions=False)

    x_h = nc.dram_tensor("x", [T, D], F32, kind="ExternalInput")
    w_h = {n: nc.dram_tensor(n, [D, D], F32, kind="ExternalInput")
           for n in ("Wq", "Wk", "Wv", "Wo", "Wf")}
    b_h = {n: nc.dram_tensor(n, [D], F32, kind="ExternalInput")
           for n in ("bq", "bk", "bv", "bo", "gamma", "beta")}
    y_h = nc.dram_tensor("y", [T, D], F32, kind="ExternalOutput")
    cc_in = nc.dram_tensor("cc_in", [1, 8], F32)
    cc_out = nc.dram_tensor("cc_out", [1, 8], F32)
    x_ap, y_ap = x_h.ap(), y_h.ap()

    tap_aps = {}
    def mktap(name, shp, dt=F32):
        if name in taps:
            tap_aps[name] = nc.dram_tensor("tap_" + name, shp, dt,
                                           kind="ExternalOutput").ap()
        return tap_aps.get(name)

    def row(h):
        return h.ap().rearrange("(a d) -> a d", a=1)

    st = ExitStack()
    perm = Region(nc)
    st.callback(perm.close)

    stopped = [False]


    sD = Ctr(st.enter_context(nc.semaphore("sD")))
    sG = Ctr(st.enter_context(nc.semaphore("sG")))
    sT = Ctr(st.enter_context(nc.semaphore("sT")))
    sV = Ctr(st.enter_context(nc.semaphore("sV")))
    sS = Ctr(st.enter_context(nc.semaphore("sS")))
    sC = Ctr(st.enter_context(nc.semaphore("sC")))
    sW = Ctr(st.enter_context(nc.semaphore("sW")))
    sF = [Ctr(st.enter_context(nc.semaphore(f"sF{i}"))) for i in range(2)]
    sX = [Ctr(st.enter_context(nc.semaphore(f"sX{i}"))) for i in range(2)]
    sO = [Ctr(st.enter_context(nc.semaphore(f"sO{i}"))) for i in range(2)]
    sCC = Ctr(st.enter_context(nc.semaphore("sCC")))

    def _stop(point, waits, dumps):
        if stop_after != point:
            return False
        stopped[0] = True
        with nc.Block() as blk:
            @blk.gpsimd
            def _(e):
                for h, v in waits:
                    e.wait_ge(h, v)
                for src, r0 in dumps:
                    rows, cols = src.shape[0], int(np.prod(src.shape[1:]))
                    sG.inc(e.dma_start(out=y_ap[r0:r0+rows, 0:cols], in_=src,
                                       allow_slow_non_contiguous=True), 16)
                e.wait_ge(sG.h, sG.n)
        return True

    i128b = perm.sb("i128b", [128, 128], BF16)
    i128f = perm.sb("i128f", [128, 128], F32)
    ones_rb = perm.sb("ones_rb", [128, 512], BF16)
    ones_cb = perm.sb("ones_cb", [128, 1], BF16)
    ones_cf = perm.sb("ones_cf", [128, 1], F32)
    ones128b = perm.sb("ones128b", [128, 128], BF16)
    ones_rf = perm.sb("ones_rf", [128, 128], F32)
    iota_r = perm.sb("iota_r", [128, 128], F32)
    iota_c = perm.sb("iota_c", [128, 1], F32)
    iota2d = perm.sb("iota2d", [128, 16], F32)
    iota38 = perm.sb("iota38", [128, 38], F32)
    iotaC2 = perm.sb("iotaC2", [128, 32], BF16)
    iotaC2f = perm.sb("iotaC2f", [128, 32], F32)

    rowpk = perm.sb("rowpk", [128, 1024], F32)
    R_GA, R_BE, R_BO, R_IDX, R_RECIP = 0, 32, 64, 0, 0
    rowpb = perm.sb("rowpb", [128, 1024], BF16)
    cc_sb = perm.sb("cc_sb", [1, 8], F32)
    srow = perm.sb("srow", [1, 16], F32)
    B_BQ, B_BK, B_BV = 0, 32, 64

    st16 = perm.sb("st16", [128, 96])
    SL = lambda k: st16[:, 16 * k:16 * (k + 1)]
    S_SUM, S_SSQ, S_MU, S_RSTD, S_NMR, S_TMP = range(6)
    amax = perm.sb("amax", [128, 16])
    inv_col = perm.sb("inv_col", [128, 1])
    sa_col = perm.sb("sa_col", [128, 1])
    dcol = perm.sb("dcol", [128, 1])
    wstat = perm.sb("wstat", [128, 24])
    idxcol = perm.sb("idxcol", [128, 8])
    idx_bc = perm.sb("idx_bc", [128, 1024])
    mx8 = perm.sb("mx8", [16, 48])
    maskT = perm.sb("maskT", [128, 256], BF16)
    rank_sb = perm.sb("rank_sb", [128, 256])
    qselT = [perm.sb(f"qselT{j}", [128, 76], BF16) for j in range(NP)]
    soT = [perm.sb(f"soT{j}", [128, 76], BF16) for j in range(NP)]
    fscr = [perm.sb(f"fscr{r}", [128, 1024]) for r in range(2)]
    qiT = perm.sb("qiT", [128, NI * 2048], BF16)

    rA = Region(nc)
    arena = [rA.sb(f"ar{m}", [128, 1024]) for m in range(NT)]
    rGB = Region(nc)
    gb_bc = rGB.sb("gb_bc", [128, 2048])

    x1 = [0] * NT
    with nc.Block() as blk:
        @blk.sync
        def _(e):
            for m in range(NT):
                sD.inc(e.dma_start(out=arena[m][:], in_=x_ap[128*m:128*(m+1), :]), 16)
                x1[m] = sD.n
            sD.inc(e.dma_start(out=rowpk[R_GA:R_GA+1, :], in_=row(b_h["gamma"])), 16)
            sD.inc(e.dma_start(out=rowpk[R_BE:R_BE+1, :], in_=row(b_h["beta"])), 16)
            sD.inc(e.dma_start(out=rowpk[R_BO:R_BO+1, :], in_=row(b_h["bo"])), 16)
        d_rows = sD.n

        @blk.gpsimd
        def _(e):
            kw = dict(allow_small_or_imprecise_dtypes=True)
            e.iota(iota_r[:], [[1, 128]], base=0, channel_multiplier=0, **kw)
            e.iota(iota_c[:], [[1, 1]], base=0, channel_multiplier=1, **kw)
            e.iota(iota2d[:], [[128, 16]], base=0, channel_multiplier=1, **kw)
            e.iota(iota38[:], [[1, 38]], base=1, channel_multiplier=0, **kw)
            e.iota(iotaC2f[:, 0:32:2], [[0, 16]], base=0, channel_multiplier=1, **kw)
            e.iota(iotaC2f[:, 1:32:2], [[1, 16]], base=0, channel_multiplier=0, **kw)
            e.memset(ones_rb[:], 1.0)
            e.memset(ones_cb[:], 1.0)
            e.memset(ones_cf[:], 1.0)
            e.memset(ones128b[:], 1.0)
            e.memset(ones_rf[:], 1.0)
            for j in range(NP):
                e.memset(qselT[j][:], 0.0)
                e.memset(soT[j][:], 0.0)
            e.memset(cc_sb[:], 0.0)
            sW.inc(e.memset(idxcol[:], -1.0))
            sG.inc(e.dma_start(out=rowpb[B_BQ:B_BQ+1, :], in_=row(b_h["bq"])), 16)
            sG.inc(e.dma_start(out=rowpb[B_BK:B_BK+1, :], in_=row(b_h["bk"])), 16)
            sG.inc(e.dma_start(out=rowpb[B_BV:B_BV+1, :], in_=row(b_h["bv"])), 16)
        g_consts = sW.n

        @blk.vector
        def _(e):
            e.wait_ge(sW.h, g_consts)
            e.tensor_scalar(i128b[:], iota_r[:], iota_c[:], None, ALU.is_equal)
            e.tensor_scalar(i128f[:], iota_r[:], iota_c[:], None, ALU.is_equal)
            sV.inc(e.tensor_copy(iotaC2[:], iotaC2f[:]))
        v_consts = sV.n

    r1 = Region(nc)
    pB = [r1.ps(f"pB{r}", [128, 512]) for r in range(4)]
    with nc.Block() as blk:
        @blk.tensor
        def _(e):
            e.wait_ge(sD.h, d_rows)
            e.wait_ge(sW.h, g_consts)
            k = 0
            for ri in (R_GA, R_BE):
                for n in range(2):
                    sT.inc(e.matmul(pB[k][:, 0:512], ones_rf[ri:ri+1, :],
                                    rowpk[ri:ri+1, 512*n:512*(n+1)],
                                    start=True, stop=True))
                    k += 1
        t_sp1 = sT.n

        @blk.vector
        def _(e):
            e.wait_ge(sT.h, t_sp1)
            e.tensor_copy(gb_bc[:, 0:512], pB[0][:, 0:512])
            e.tensor_copy(gb_bc[:, 512:1024], pB[1][:, 0:512])
            e.tensor_copy(gb_bc[:, 1024:1536], pB[2][:, 0:512])
            sV.inc(e.tensor_copy(gb_bc[:, 1536:2048], pB[3][:, 0:512]))
        v_sp1 = sV.n
    r1.close()

    FBLK_cm = nc.Block()
    FBLK = FBLK_cm.__enter__()
    if True:
        blk = FBLK
        @blk.vector
        def _(e):
            for m in range(NT):
                e.wait_ge(sD.h, x1[m])
                instr = e.tensor_reduce(SL(S_SUM)[:, m:m+1], arena[m][:], AX.X, ALU.add)
            sV.inc(instr)
            e.wait_ge(sV.h, sV.n)
            sV.inc(e.tensor_scalar(SL(S_MU), SL(S_SUM), pf(1.0 / D), None, ALU.mult))
            e.wait_ge(sV.h, sV.n)
            sV.inc(e.tensor_tensor(SL(S_TMP), SL(S_MU), SL(S_MU), ALU.mult))
        v_a = sV.n

        @blk.scalar
        def _(e):
            for m in range(NT):
                e.wait_ge(sD.h, x1[m])
                e.activation(fscr[1][:], arena[m][:], AF.Square,
                             accum_out=SL(S_SSQ)[:, m:m+1])
            sS.inc(e.copy(fscr[1][:, 2:3], arena[0][:, 0:1]))
        s_a = sS.n

        @blk.vector
        def _(e):
            e.wait_ge(sS.h, s_a)
            sV.inc(e.scalar_tensor_tensor(SL(S_SSQ), SL(S_SSQ), pf(1.0 / D), SL(S_TMP),
                                          ALU.mult, ALU.subtract))
            e.wait_ge(sV.h, sV.n)
            sV.inc(e.tensor_scalar(SL(S_SSQ), SL(S_SSQ), pf(1e-5), None, ALU.add))
        v_b = sV.n

        @blk.scalar
        def _(e):
            e.wait_ge(sV.h, v_b)
            sS.inc(e.activation(SL(S_TMP), SL(S_SSQ), AF.Sqrt))
        s_b = sS.n

        @blk.vector
        def _(e):
            e.wait_ge(sS.h, s_b)
            sV.inc(e.reciprocal(SL(S_RSTD), SL(S_TMP)))
            e.wait_ge(sV.h, sV.n)
            sV.inc(e.scalar_tensor_tensor(SL(S_NMR), SL(S_MU), pf(-1.0), SL(S_RSTD),
                                          ALU.mult, ALU.mult))
        v_stats1 = sV.n

    if _stop("stats", [(sV.h, v_stats1), (sS.h, s_a)],
             [(st16[:], 0), (arena[3][:], 128), (arena[0][:], 256)]):
        return nc, tap_aps

    rXB = Region(nc)
    xnb = rXB.sb("xnb", [128, NT * 1024], BF16)
    if True:
        blk = FBLK
        @blk.scalar
        def _(e):
            e.wait_ge(sV.h, v_stats1)
            for m in range(NT):
                sS.inc(e.activation(arena[m][:], arena[m][:], AF.Identity,
                                    scale=SL(S_RSTD)[:, m:m+1], bias=SL(S_NMR)[:, m:m+1]))
        s_xn0 = sS.n - NT

        @blk.vector
        def _(e):
            for m in range(NT):
                e.wait_ge(sS.h, s_xn0 + m + 1)
                sV.inc(e.tensor_copy(xnb[:, 1024*m:1024*(m+1)], arena[m][:]))
        v_xnb0 = sV.n - NT

    if _stop("xn", [(sS.h, s_xn0 + NT)],
             [(arena[m][:], 128 * m) for m in range(NT)]):
        return nc, tap_aps

    if mktap("xn", [128, 1024]):
        with nc.Block() as blk:
            @blk.sync
            def _(e):
                e.wait_ge(sS.h, s_xn0 + 4)
                sD.inc(e.dma_start(out=tap_aps["xn"], in_=arena[3][:]), 16)

    rXT = Region(nc)
    xnT = rXT.sb("xnT", [128, NI * 2048], BF16, side="right")
    r6 = Region(nc)
    pT6 = [r6.ps(f"pT6{r}", [128, 512]) for r in range(2)]
    if True:
        blk = FBLK
        t6, s6 = sT.n, sS.n

        @blk.tensor
        def _(e):
            e.wait_ge(sV.h, v_sp1)
            g = 0
            for m in range(NT):
                e.wait_ge(sV.h, v_xnb0 + m + 1)
                for a in range(2):
                    if g >= 2:
                        e.wait_ge(sS.h, s6 + (g - 2) + 1)
                    for i in range(4):
                        ic = 4 * a + i
                        instr = e.matmul(pT6[g % 2][:, 128*i:128*(i+1)],
                                         xnb[:, 1024*m + 128*ic:1024*m + 128*(ic+1)],
                                         i128b[:], start=True, stop=True)
                        if i == 3:
                            sT.inc(instr)
                    g += 1

        @blk.scalar
        def _(e):
            for g in range(2 * NT):
                m, a = g // 2, g % 2
                e.wait_ge(sT.h, t6 + g + 1)
                dst = xnT[:].rearrange("p (c t) -> p c t", t=2048)[:, 4*a:4*a+4, 128*m:128*(m+1)]
                src = pT6[g % 2][:].rearrange("p (c t) -> p c t", t=128)
                sS.inc(e.copy(dst, src))
        s_xnT = sS.n
    r6.close()
    if _stop("xnT", [(sS.h, s_xnT)],
             [(xnT[:, 1024*k:1024*(k+1)], 128 * k) for k in range(16)]):
        return nc, tap_aps
    rXB.close()


    if True:
        blk = FBLK
        @blk.vector
        def _(e):
            for m in range(NT):
                e.wait_ge(sS.h, s_xn0 + m + 1)
                instr = e.tensor_reduce(SL(S_SUM)[:, m:m+1], arena[m][:], AX.X, ALU.add)
            sV.inc(instr)
            e.wait_ge(sV.h, sV.n)
            sV.inc(e.tensor_scalar(SL(S_MU), SL(S_SUM), pf(1.0 / D), None, ALU.mult))
            e.wait_ge(sV.h, sV.n)
            sV.inc(e.tensor_scalar(SL(S_SUM), SL(S_SUM), pf(-1.0 / D), None, ALU.mult))
        v_c = sV.n

        @blk.scalar
        def _(e):
            e.wait_ge(sV.h, v_c)
            for m in range(NT):
                e.activation(fscr[1][:], arena[m][:], AF.Square,
                             bias=SL(S_SUM)[:, m:m+1],
                             accum_out=SL(S_SSQ)[:, m:m+1])
            sS.inc(e.copy(fscr[1][:, 2:3], arena[0][:, 0:1]))
        s_c = sS.n

        @blk.vector
        def _(e):
            e.wait_ge(sS.h, s_c)
            sV.inc(e.tensor_scalar(SL(S_SSQ), SL(S_SSQ), pf(1.0 / D), pf(1e-5),
                                   ALU.mult, ALU.add))
        v_d = sV.n

        @blk.scalar
        def _(e):
            e.wait_ge(sV.h, v_d)
            sS.inc(e.activation(SL(S_TMP), SL(S_SSQ), AF.Sqrt))
        s_d = sS.n

        @blk.vector
        def _(e):
            e.wait_ge(sS.h, s_d)
            sV.inc(e.reciprocal(SL(S_RSTD), SL(S_TMP)))
            e.wait_ge(sV.h, sV.n)
            sV.inc(e.scalar_tensor_tensor(SL(S_NMR), SL(S_MU), pf(-1.0), SL(S_RSTD),
                                          ALU.mult, ALU.mult))
        v_stats2 = sV.n

        @blk.scalar
        def _(e):
            e.wait_ge(sV.h, v_stats2)
            for m in range(NT):
                sS.inc(e.activation(arena[m][:], arena[m][:], AF.Identity,
                                    scale=SL(S_RSTD)[:, m:m+1], bias=SL(S_NMR)[:, m:m+1]))
        s_s1 = sS.n - NT

        @blk.vector
        def _(e):
            e.wait_ge(sV.h, v_sp1)
            for m in range(NT):
                e.wait_ge(sS.h, s_s1 + m + 1)
                e.tensor_tensor(arena[m][:], arena[m][:], gb_bc[:, 0:1024], ALU.mult)
                e.tensor_tensor(arena[m][:], arena[m][:], gb_bc[:, 1024:2048], ALU.add)
                sV.inc(e.tensor_reduce(amax[:, m:m+1], arena[m][:], AX.X, ALU.max,
                                       apply_absolute_value=True))
        v_xn2 = sV.n

    if _stop("xn2", [(sV.h, v_xn2)],
             [(arena[m][:], 128 * m) for m in range(NT)]):
        return nc, tap_aps
    rGB.close()
    rKTa = Region(nc)
    kTa = rKTa.sb("kTa", [128, 3 * 2048], BF16, side="right")
    rWK = Region(nc)
    wkb = rWK.sb("wkb", [128, NI * 1024], BF16, side="right")
    rWQ = Region(nc)
    wqb = rWQ.sb("wqb", [128, NI * 1024], BF16, side="right")
    if True:
        blk = FBLK
        @blk.gpsimd
        def _(e):
            e.wait_ge(sS.h, s_xnT)
            for i in range(NI):
                sG.inc(e.dma_start(out=wkb[:, 1024*i:1024*(i+1)],
                                   in_=w_h["Wk"].ap()[128*i:128*(i+1), :]), 16)
        g_wk = sG.n

        @blk.gpsimd
        def _(e):
            for i in range(NI):
                sG.inc(e.dma_start(out=wqb[:, 1024*i:1024*(i+1)],
                                   in_=w_h["Wq"].ap()[128*i:128*(i+1), :]), 16)
        g_wq = sG.n


    if mktap("xn2", [128, 1024]):
        with nc.Block() as blk:
            @blk.sync
            def _(e):
                e.wait_ge(sV.h, v_xn2)
                sD.inc(e.dma_start(out=tap_aps["xn2"], in_=arena[3][:]), 16)

    r8 = Region(nc)
    p8a = r8.ps("p8a", [1, 128])
    p8b = r8.ps("p8b", [128, 1])
    if True:
        blk = FBLK
        @blk.vector
        def _(e):
            sV.inc(e.tensor_reduce(SL(S_TMP)[:, 0:1], amax[:], AX.X, ALU.max))
        v_e = sV.n

        @blk.tensor
        def _(e):
            e.wait_ge(sV.h, v_e)
            e.wait_ge(sS.h, s_xnT)
            sT.inc(e.matmul(p8a[:], SL(S_TMP)[:, 0:1], i128f[:], start=True, stop=True))
        t_e = sT.n

    r13a = Region(nc)
    pK1 = [r13a.ps(f"pK1{r}", [128, 512]) for r in range(4)]
    if True:
        blk = FBLK
        t13a, s13a = sT.n, sS.n

        @blk.tensor
        def _(e):
            e.wait_ge(sG.h, g_wk)
            e.wait_ge(sS.h, s_xnT)
            for jc in range(3):
                for n4 in range(4):
                    g = 4 * jc + n4
                    if g >= 4:
                        e.wait_ge(sS.h, s13a + (g - 4) + 1)
                    for ic in range(NI):
                        e.matmul(pK1[g % 4][:, 0:512],
                                 wkb[:, 1024*ic + 128*jc:1024*ic + 128*(jc+1)],
                                 xnT[:, 2048*ic + 512*n4:2048*ic + 512*(n4+1)],
                                 start=(ic == 0), stop=False)
                    sT.inc(e.matmul(pK1[g % 4][:, 0:512],
                                    rowpb[B_BK:B_BK+1, 128*jc:128*(jc+1)],
                                    ones_rb[B_BK:B_BK+1, 0:512], start=False, stop=True))

        @blk.scalar
        def _(e):
            for g in range(12):
                jc, n4 = g // 4, g % 4
                e.wait_ge(sT.h, t13a + g + 1)
                sS.inc(e.copy(kTa[:, 2048*jc + 512*n4:2048*jc + 512*(n4+1)],
                              pK1[g % 4][:, 0:512]))
        s_kTa = sS.n
    r13a.close()

    if True:
        blk = FBLK

        @blk.vector
        def _(e):
            e.wait_ge(sT.h, t_e)
            sV.inc(e.tensor_reduce(srow[0:1, 0:1], p8a[0:1, :], AX.X, ALU.max))
            e.wait_ge(sV.h, sV.n)
            sV.inc(e.tensor_copy(cc_sb[0:1, 0:1], srow[0:1, 0:1]))
            e.wait_ge(sV.h, sV.n)
        v_f = sV.n

        @blk.sync
        def _(e):
            e.wait_ge(sV.h, v_f)
            sCC.inc(e.dma_start(out=cc_in.ap(), in_=cc_sb[:]), 16)

        @blk.gpsimd
        def _(e):
            e.wait_ge(sCC.h, 16)
            if stop_after == "ccin":
                sG.inc(e.dma_start(out=fscr[0][0:1, 0:128], in_=cc_in.ap()), 16)
                e.wait_ge(sG.h, sG.n)
                sG.inc(e.dma_start(out=y_ap[0:1, 0:128], in_=fscr[0][0:1, 0:128]), 16)
                sG.inc(e.dma_start(out=y_ap[1:2, 0:128], in_=cc_sb[:]), 16)
                e.wait_ge(sG.h, sG.n)
            sC.inc(e.collective_compute(
                "AllReduce", ALU.max, replica_groups=[list(range(8))],
                ins=[cc_in.ap().opt()], outs=[cc_out.ap().opt()]))
        c_cc = sC.n

        @blk.sync
        def _(e):
            e.wait_ge(sC.h, c_cc)
            sCC.inc(e.dma_start(out=cc_sb[:], in_=cc_out.ap()), 16)
        g_cc = sCC.n

        @blk.vector
        def _(e):
            e.wait_ge(sCC.h, g_cc)
            sV.inc(e.tensor_scalar(srow[0:1, 1:2], cc_sb[0:1, 0:1],
                                   pf(1e-8), pf(1.0 / 127.0), ALU.max, ALU.mult))
            e.wait_ge(sV.h, sV.n)
            sV.inc(e.reciprocal(srow[0:1, 2:3], srow[0:1, 1:2]))
        v_h = sV.n

        @blk.tensor
        def _(e):
            e.wait_ge(sV.h, v_h)
            sT.inc(e.matmul(p8b[:], ones_rf[0:1, :],
                            srow[0:1, 2:3], start=True, stop=True))
        t_f = sT.n

        @blk.vector
        def _(e):
            e.wait_ge(sT.h, t_f)
            sV.inc(e.tensor_copy(inv_col[:], p8b[:]))
        v_inv = sV.n
    r8.close()

    if _stop("scale", [(sV.h, v_inv)],
             [(srow[0:1, 0:3], 0), (cc_sb[:], 1), (inv_col[:], 2), (amax[:], 4)]):
        return nc, tap_aps


    FBLK_cm.__exit__(None, None, None)
    FBLK_cm = nc.Block()
    FBLK = FBLK_cm.__enter__()

    r9 = Region(nc)
    i32s = r9.sb("i32s", [128, 512], I32)
    qib = [r9.sb(f"qib{r}", [128, 1024], BF16) for r in range(2)]
    pT9 = [r9.ps(f"pT9{r}", [128, 512]) for r in range(2)]
    if True:
        blk = FBLK
        v9b, s9, t9 = sV.n, sS.n, sT.n

        def qib_done(m):
            return s9 + 4 * m + 4

        @blk.scalar
        def _(e):
            e.wait_ge(sV.h, v_inv)
            for m in range(NT):
                if m >= 2:
                    e.wait_ge(sT.h, t9 + 2*m - 2)
                for h in range(2):
                    sS.inc(e.activation(i32s[:], arena[m][:, 512*h:512*(h+1)],
                                        AF.Identity, scale=inv_col[:, 0:1]))
                    sS.inc(e.copy(qib[m % 2][:, 512*h:512*(h+1)], i32s[:]))
        s_casts = sS.n

        @blk.vector
        def _(e):
            for g in range(2 * NT):
                e.wait_ge(sT.h, t9 + g + 1)
                mm, aa = g // 2, g % 2
                dst = qiT[:].rearrange("p (c t) -> p c t", t=2048)[:, 4*aa:4*aa+4, 128*mm:128*(mm+1)]
                src = pT9[g % 2][:].rearrange("p (c t) -> p c t", t=128)
                sV.inc(e.tensor_copy(dst, src))
        s_qiT = sS.n
        v_qiT = sV.n

        @blk.tensor
        def _(e):
            for m in range(NT):
                e.wait_ge(sS.h, qib_done(m))
                for a in range(2):
                    g = 2 * m + a
                    if g >= 2:
                        e.wait_ge(sV.h, v9b + (g - 2) + 1)
                    for i in range(4):
                        ic = 4 * a + i
                        instr = e.matmul(pT9[g % 2][:, 128*i:128*(i+1)],
                                         qib[m % 2][:, 128*ic:128*(ic+1)],
                                         i128b[:], start=True, stop=True)
                        if i == 3:
                            sT.inc(instr)
    if _stop("qiT", [(sS.h, s_qiT)],
             [(qiT[:, 1024*k:1024*(k+1)], 128 * k) for k in range(16)]):
        return nc, tap_aps
    r9.close()
    rA.close()

    if mktap("qint", [128, 1024], BF16):
        with nc.Block() as blk:
            @blk.sync
            def _(e):
                e.wait_ge(sS.h, s_qiT)
                sD.inc(e.dma_start(out=tap_aps["qint"], in_=qib[1][:]), 16)
    if mktap("qiT", [128, 2048], BF16):
        with nc.Block() as blk:
            @blk.sync
            def _(e):
                e.wait_ge(sS.h, s_qiT)
                sD.inc(e.dma_start(out=tap_aps["qiT"], in_=qiT[:, 0:2048]), 16)

    FBLK_cm.__exit__(None, None, None)
    MBLK = st.enter_context(nc.Block())
    rKT = Region(nc)
    kT = rKT.sb("kT", [128, 5 * 2048], BF16)
    rQ = Region(nc)
    qbf = [rQ.sb(f"qbf{m}", [128, 1024], BF16) for m in range(NT)]
    rQN = Region(nc)
    QN = rQN.sb("QN", [128, 256])
    r10 = Region(nc)
    pA = [r10.ps(f"pA{r}", [128, 512]) for r in range(4)]
    if True:
        blk = MBLK
        t10, s10 = sT.n, sS.n

        @blk.tensor
        def _(e):
            e.wait_ge(sG.h, g_wq)
            e.wait_ge(sV.h, v_qiT)
            for m in range(NT):
                for n in range(2):
                    g = 2 * m + n
                    if g >= 4:
                        e.wait_ge(sS.h, s10 + (g - 4) + 1)
                    for ic in range(NI):
                        e.matmul(pA[g % 4][:, 0:512],
                                 xnT[:, 2048*ic + 128*m:2048*ic + 128*(m+1)],
                                 wqb[:, 1024*ic + 512*n:1024*ic + 512*(n+1)],
                                 start=(ic == 0), stop=False)
                    sT.inc(e.matmul(pA[g % 4][:, 0:512], ones_rb[0:1, 0:128],
                                    rowpb[B_BQ:B_BQ+1, 512*n:512*(n+1)],
                                    start=False, stop=True))

        @blk.scalar
        def _(e):
            for g in range(2 * NT):
                m, n = g // 2, g % 2
                e.wait_ge(sT.h, t10 + g + 1)
                sS.inc(e.copy(qbf[m][:, 512*n:512*(n+1)], pA[g % 4][:, 0:512]))

        @blk.vector
        def _(e):
            for m in range(NT):
                e.wait_ge(sS.h, s10 + 2 * m + 2)
                sV.inc(e.tensor_reduce(
                    QN[:, 16*m:16*(m+1)],
                    qbf[m][:].rearrange("p (h d) -> p h d", d=64),
                    AX.X, ALU.add, apply_absolute_value=True))
        v_qn = sV.n
    if _stop("q", [(sS.h, s10 + 2 * NT)],
             [(qbf[m][:], 128 * m) for m in range(NT)]):
        return nc, tap_aps
    if _stop("qn", [(sV.h, v_qn)], [(QN[:], 0)]):
        return nc, tap_aps
    r10.close()
    rWQ.close()

    if mktap("q3", [128, 1024], BF16):
        with nc.Block() as blk:
            @blk.sync
            def _(e):
                e.wait_ge(sV.h, v_qn)
                sD.inc(e.dma_start(out=tap_aps["q3"], in_=qbf[3][:]), 16)
    if mktap("qn", [128, 256]):
        with nc.Block() as blk:
            @blk.sync
            def _(e):
                e.wait_ge(sV.h, v_qn)
                sD.inc(e.dma_start(out=tap_aps["qn"], in_=QN[:]), 16)

    r13 = Region(nc)
    pK = [r13.ps(f"pK{r}", [128, 512]) for r in range(4)]
    if True:
        blk = MBLK
        t13, s13 = sT.n, sS.n

        @blk.tensor
        def _(e):
            e.wait_ge(sG.h, g_wk)
            e.wait_ge(sS.h, s10 + 2 * NT)
            for jc in range(3, NI):
                for n4 in range(4):
                    g = 4 * (jc - 3) + n4
                    if g >= 4:
                        e.wait_ge(sS.h, s13 + (g - 4) + 1)
                    for ic in range(NI):
                        e.matmul(pK[g % 4][:, 0:512],
                                 wkb[:, 1024*ic + 128*jc:1024*ic + 128*(jc+1)],
                                 xnT[:, 2048*ic + 512*n4:2048*ic + 512*(n4+1)],
                                 start=(ic == 0), stop=False)
                    sT.inc(e.matmul(pK[g % 4][:, 0:512],
                                    rowpb[B_BK:B_BK+1, 128*jc:128*(jc+1)],
                                    ones_rb[B_BK:B_BK+1, 0:512], start=False, stop=True))

        @blk.scalar
        def _(e):
            for g in range(20):
                jc, n4 = 3 + g // 4, g % 4
                e.wait_ge(sT.h, t13 + g + 1)
                sS.inc(e.copy(kT[:, 2048*(jc-3) + 512*n4:2048*(jc-3) + 512*(n4+1)],
                              pK[g % 4][:, 0:512]))
        s_kT = sS.n
    if _stop("kT", [(sS.h, s_kT)],
             [(kT[:, 1024*k:1024*(k+1)], 128 * k) for k in range(8)]):
        return nc, tap_aps
    r13.close()
    rWK.close()

    t_K = sT.n
    rVB = Region(nc)
    vbf = rVB.sb("vbf", [128, NT * 1024], BF16, side="right")
    rWV = Region(nc)
    wvb = rWV.sb("wvb", [128, NI * 1024], BF16, side="right")
    if True:
        blk = MBLK
        @blk.gpsimd
        def _(e):
            e.wait_ge(sT.h, t_K)
            for i in range(NI):
                sG.inc(e.dma_start(out=wvb[:, 1024*i:1024*(i+1)],
                                   in_=w_h["Wv"].ap()[128*i:128*(i+1), :]), 16)
        g_wv = sG.n

    r14 = Region(nc)
    pV = [r14.ps(f"pV{r}", [128, 512]) for r in range(4)]
    if True:
        blk = MBLK
        t14, s14 = sT.n, sS.n

        @blk.tensor
        def _(e):
            e.wait_ge(sG.h, g_wv)
            e.wait_ge(sS.h, s_kT)
            for m in range(NT):
                for n in range(2):
                    g = 2 * m + n
                    if g >= 4:
                        e.wait_ge(sS.h, s14 + (g - 4) + 1)
                    for ic in range(NI):
                        e.matmul(pV[g % 4][:, 0:512],
                                 xnT[:, 2048*ic + 128*m:2048*ic + 128*(m+1)],
                                 wvb[:, 1024*ic + 512*n:1024*ic + 512*(n+1)],
                                 start=(ic == 0), stop=False)
                    sT.inc(e.matmul(pV[g % 4][:, 0:512], ones_rb[B_BV:B_BV+1, 0:128],
                                    rowpb[B_BV:B_BV+1, 512*n:512*(n+1)],
                                    start=False, stop=True))

        @blk.scalar
        def _(e):
            for g in range(2 * NT):
                m, n = g // 2, g % 2
                e.wait_ge(sT.h, t14 + g + 1)
                sS.inc(e.copy(vbf[:, 1024*m + 512*n:1024*m + 512*(n+1)],
                              pV[g % 4][:, 0:512]))
        s_vbf = sS.n
    r14.close()
    rWV.close()

    r11 = Region(nc)
    QNW = r11.sb("QNW", [16, 2048])
    thr = r11.sb("thr", [16, 2048], BF16)
    thr2 = r11.sb("thr2", [16, 2048], BF16)
    u128b = r11.sb("u128b", [128, 128], BF16)
    pQNT = r11.ps("pQNT", [16, 2048])
    if True:
        blk = MBLK
        @blk.tensor
        def _(e):
            e.wait_ge(sV.h, v_qn)
            e.wait_ge(sS.h, s_vbf)
            for c in range(NT):
                sT.inc(e.matmul(pQNT[:, 128*c:128*(c+1)], QN[:, 16*c:16*(c+1)],
                                i128f[:], start=True, stop=True))
        t_qnt = sT.n

        @blk.vector
        def _(e):
            e.wait_ge(sT.h, t_qnt)
            e.tensor_scalar(u128b[:], iota_r[:], iota_c[:], None, ALU.is_ge)
            sV.inc(e.tensor_copy(QNW[:], pQNT[0:16, :]))
            for r in range(5):
                sV.inc(e.max(out=mx8[:, 8*r:8*(r+1)], in_=QNW[:]))
                if r < 4:
                    e.wait_ge(sV.h, sV.n)
                    sV.inc(e.scalar_tensor_tensor(QNW[:], QNW[:], mx8[:, 8*r+7:8*r+8],
                                                  QNW[:], ALU.is_lt, ALU.mult))
                    e.wait_ge(sV.h, sV.n)
            e.wait_ge(sV.h, sV.n)
            e.tensor_scalar(thr[:], QNW[:], mx8[:, 37:38], None, ALU.is_ge)
            sV.inc(e.tensor_scalar(thr2[:], QNW[:], pf(0.0), None, ALU.is_equal))
            e.wait_ge(sV.h, sV.n)
            sV.inc(e.tensor_tensor(thr[:], thr[:], thr2[:], ALU.max))
        v_thr = sV.n
    if mktap("mx8", [16, 40]):
        with nc.Block() as blk:
            @blk.sync
            def _(e):
                e.wait_ge(sV.h, v_thr)
                sD.inc(e.dma_start(out=tap_aps["mx8"], in_=mx8[:, 0:40]), 16)
    if mktap("mask", [16, 2048], BF16):
        with nc.Block() as blk:
            @blk.sync
            def _(e):
                e.wait_ge(sV.h, v_thr)
                sD.inc(e.dma_start(out=tap_aps["mask"], in_=thr[:]), 16)

    pMT = r11.ps("pMT", [128, 256])
    pRK = [r11.ps(f"pRK{r}", [128, 16]) for r in range(2)]
    if True:
        blk = MBLK
        @blk.tensor
        def _(e):
            e.wait_ge(sV.h, v_thr)
            for c in range(NT):
                sT.inc(e.matmul(pMT[:, 16*c:16*(c+1)], thr[:, 128*c:128*(c+1)],
                                i128b[0:16, 0:16], start=True, stop=True))
        t_mt = sT.n

        @blk.vector
        def _(e):
            e.wait_ge(sT.h, t_mt)
            sV.inc(e.tensor_copy(maskT[:], pMT[:]))
        v_mt = sV.n

        @blk.tensor
        def _(e):
            e.wait_ge(sV.h, v_mt)
            for c in range(NT):
                if c >= 2:
                    e.wait_ge(sV.h, v_mt + (c - 2) + 1)
                for c2 in range(c + 1):
                    instr = e.matmul(pRK[c % 2][:],
                                     u128b[:] if c2 == c else ones128b[:],
                                     maskT[:, 16*c2:16*(c2+1)],
                                     start=(c2 == 0), stop=(c2 == c))
                    if c2 == c:
                        sT.inc(instr)
        t_rk = sT.n - NT

        @blk.vector
        def _(e):
            for c in range(NT):
                e.wait_ge(sT.h, t_rk + c + 1)
                sV.inc(e.tensor_tensor(rank_sb[:, 16*c:16*(c+1)], pRK[c % 2][:],
                                       maskT[:, 16*c:16*(c+1)], ALU.mult))
        v_rk = sV.n
    if _stop("rank", [(sV.h, v_rk), (sV.h, v_thr)],
             [(rank_sb[:], 0), (maskT[:], 128), (mx8[:, 0:40], 256),
              (thr[:, 0:1024], 272), (thr[:, 1024:2048], 400)]):
        return nc, tap_aps
    r11.close()
    rQN.close()

    if mktap("rank", [128, 256]):
        with nc.Block() as blk:
            @blk.sync
            def _(e):
                e.wait_ge(sV.h, v_rk)
                sD.inc(e.dma_start(out=tap_aps["rank"], in_=rank_sb[:]), 16)

    r12 = Region(nc)
    PTr = [r12.sb(f"PTr{c}", [128, 1024], BF16) for c in range(8)]
    qhalf = r12.sb("qhalf", [128, NP * 128])
    qsel_s = r12.sb("qsel_s", [128, 128], BF16)
    ish64 = r12.sb("ish64", [128, 128], F32)
    pIdxL = r12.ps("pIdxL", [1, 1024])
    pIdxH = r12.ps("pIdxH", [1, 1024])
    r12a = Region(nc)
    pQS = [r12a.ps(f"pQS{r}", [128, 128]) for r in range(2)]
    pT12 = r12a.ps("pT12", [128, 128])

    if True:
        blk = MBLK
        @blk.gpsimd
        def _(e):
            e.wait_ge(sV.h, v_rk)
            e.wait_ge(sT.h, t_rk + NT)
            for c in range(8):
                instr = e.memset(PTr[c][:], 0.0)
            sW.inc(instr)
        g_pt0 = sW.n

    half_T = [0, 0]
    half_S = [0, 0]
    half_V = [0, 0]
    for half in range(2):
        cbase = 8 * half
        if True:
            blk = MBLK
            v12, t12, s12 = sV.n, sT.n, sS.n

            @blk.vector
            def _(e):
                e.wait_ge(sW.h, g_pt0)
                e.wait_ge(sV.h, v_rk)
                if half == 1:
                    e.wait_ge(sT.h, half_T[0])
                    e.wait_ge(sS.h, half_S[0])
                for cc in range(8):
                    c = cbase + cc
                    for s in range(2):
                        dst = PTr[cc][:].rearrange("p (j r) -> p j r", r=128)[:, :, 38*s:38*(s+1)]
                        src = rank_sb[:, 16*c + s:16*c + 16:2].unsqueeze(-1).to_broadcast([128, 8, 38])
                        io = iota38[:].unsqueeze(1).to_broadcast([128, 8, 38])
                        instr = e.tensor_tensor(dst, src, io, ALU.is_equal)
                        if s == 1:
                            sV.inc(instr)
                if half == 1:
                    for j in range(NP):
                        e.wait_ge(sT.h, (t12 + 9) if j == 0 else (t12 + 8 + 2 * j))
                        if j >= 1:
                            e.wait_ge(sT.h, t12 + 2 * j + 9)
                        sV.inc(e.scalar_tensor_tensor(
                            qsel_s[0:76, 0:128], pQS[j % 2][0:76, :], pf(0.125),
                            qhalf[0:76, 128*j:128*(j+1)], ALU.mult, ALU.add))

            @blk.tensor
            def _(e):
                for cc in range(8):
                    c = cbase + cc
                    e.wait_ge(sV.h, v12 + cc + 1)
                    for n in range(2):
                        e.matmul(pIdxL[0:1, 512*n:512*(n+1)],
                                 iotaC2[:, 2*c:2*c+1],
                                 PTr[cc][:, 512*n:512*(n+1)],
                                 start=(c == 0), stop=(c == NT - 1))
                        instr = e.matmul(pIdxH[0:1, 512*n:512*(n+1)],
                                         iotaC2[:, 2*c+1:2*c+2],
                                         PTr[cc][:, 512*n:512*(n+1)],
                                         start=(c == 0), stop=(c == NT - 1))
                    sT.inc(instr)
                def gather(j):
                    if j >= 2:
                        if half == 0:
                            e.wait_ge(sS.h, s12 + (j - 2) + 1)
                        else:
                            e.wait_ge(sV.h, v12 + 8 + (j - 2) + 1)
                    for cc in range(8):
                        c = cbase + cc
                        instr = e.matmul(pQS[j % 2][0:76, :],
                                         PTr[cc][:, 128*j:128*j + 76],
                                         qbf[c][:, 128*j:128*(j+1)],
                                         start=(cc == 0), stop=(cc == 7))
                    sT.inc(instr)
                def transpose(j):
                    e.wait_ge(sV.h, v12 + 8 + j + 1)
                    if j >= 1:
                        e.wait_ge(sS.h, s12 + j)
                    sT.inc(e.matmul(pT12[:, 0:76], qsel_s[0:76, 0:128],
                                    i128b[0:76, 0:76], start=True, stop=True))
                if half == 0:
                    for j in range(NP):
                        gather(j)
                else:
                    for j in range(NP):
                        gather(j)
                        if j >= 1:
                            transpose(j - 1)
                    transpose(NP - 1)
            half_T[half] = sT.n

            if half == 0:
                @blk.scalar
                def _(e):
                    for j in range(NP):
                        e.wait_ge(sT.h, t12 + 8 + j + 1)
                        sS.inc(e.activation(qhalf[0:76, 128*j:128*(j+1)],
                                            pQS[j % 2][0:76, :], AF.Identity,
                                            scale=f32(0.125)))
            else:
                @blk.scalar
                def _(e):
                    for j in range(NP):
                        e.wait_ge(sT.h, (t12 + 2 * j + 11) if j < NP - 1 else (t12 + 24))
                        e.copy(qselT[j][0:64, 0:38], pT12[0:64, 0:38])
                        sS.inc(e.copy(qselT[j][64:128, 38:76], pT12[64:128, 38:76]))
            half_S[half] = sS.n
            half_V[half] = sV.n
    s_gather = sS.n

    r12a.close()
    pIB = [r12.ps(f"pIB{r}", [128, 512]) for r in range(2)]
    if True:
        blk = MBLK
        @blk.vector
        def _(e):
            e.wait_ge(sT.h, half_T[1])
            e.tensor_copy(fscr[0][0:1, :], pIdxL[0:1, :])
            sV.inc(e.tensor_copy(fscr[1][0:1, :], pIdxH[0:1, :]))
            e.wait_ge(sV.h, sV.n)
            sV.inc(e.scalar_tensor_tensor(rowpk[R_IDX:R_IDX+1, :], fscr[1][0:1, :],
                                          pf(128.0), fscr[0][0:1, :], ALU.mult, ALU.add))
        v_idx = sV.n

        @blk.tensor
        def _(e):
            e.wait_ge(sV.h, v_idx)
            for n in range(2):
                sT.inc(e.matmul(pIB[n][:, 0:512], ones_rf[R_IDX:R_IDX+1, :],
                                rowpk[R_IDX:R_IDX+1, 512*n:512*(n+1)],
                                start=True, stop=True))
        t_ib = sT.n

        @blk.vector
        def _(e):
            e.wait_ge(sT.h, t_ib)
            e.tensor_copy(idx_bc[:, 0:512], pIB[0][:, 0:512])
            sV.inc(e.tensor_copy(idx_bc[:, 512:1024], pIB[1][:, 0:512]))
            sV.inc(e.tensor_scalar(ish64[:], iota_r[:], pf(64.0), iota_c[:],
                                   ALU.add, ALU.is_equal))
            e.wait_ge(sV.h, sV.n)
            for j in range(NP):
                sV.inc(e.tensor_tensor(fscr[0][0:38, 0:38], idx_bc[0:38, 128*j:128*j+38],
                                       i128f[0:38, 0:38], ALU.mult))
                e.wait_ge(sV.h, sV.n)
                sV.inc(e.tensor_reduce(idxcol[0:38, j:j+1], fscr[0][0:38, 0:38], AX.X, ALU.add))
                sV.inc(e.tensor_tensor(fscr[0][64:102, 0:38], idx_bc[64:102, 128*j+38:128*j+76],
                                       ish64[64:102, 0:38], ALU.mult))
                e.wait_ge(sV.h, sV.n)
                sV.inc(e.tensor_reduce(idxcol[64:102, j:j+1], fscr[0][64:102, 0:38],
                                       AX.X, ALU.add))
        v_idxc = sV.n
    if _stop("sel", [(sS.h, s_gather), (sV.h, v_idxc)],
             [(idx_bc[:], 0), (idxcol[:], 128)] +
             [(qselT[j][:], 256 + 128 * j) for j in range(NP)]):
        return nc, tap_aps
    r12.close()
    rQ.close()

    if mktap("idxbc", [128, 1024]):
        with nc.Block() as blk:
            @blk.sync
            def _(e):
                e.wait_ge(sV.h, v_idxc)
                sD.inc(e.dma_start(out=tap_aps["idxbc"], in_=idx_bc[:]), 16)
    if mktap("idxcol", [128, 8]):
        with nc.Block() as blk:
            @blk.sync
            def _(e):
                e.wait_ge(sV.h, v_idxc)
                sD.inc(e.dma_start(out=tap_aps["idxcol"], in_=idxcol[:]), 16)
    if mktap("qselT0", [128, 76], BF16):
        with nc.Block() as blk:
            @blk.sync
            def _(e):
                e.wait_ge(sS.h, s_gather)
                sD.inc(e.dma_start(out=tap_aps["qselT0"], in_=qselT[0][:]), 16)

    rAT = Region(nc)
    attn = [rAT.sb(f"attn{c}", [128, 1024], BF16, side="right") for c in range(NT)]
    maskC = [rAT.sb(f"maskC{r}", [128, 1024], BF16, side="right") for r in range(2)]
    rPS2 = Region(nc)
    pRS = rPS2.ps("pRS", [1, 1024])
    rPS1 = Region(nc)
    pS15 = [rPS1.ps(f"pS15{r}", [128, 1024]) for r in range(2)]
    if True:
        blk = MBLK
        t15, s15, v15 = sT.n, sS.n, sV.n

        @blk.vector
        def _(e):
            e.wait_ge(sV.h, v_idxc)
            for r in range(2):
                ph = pS15[r][:].rearrange("p (j w) -> p j w", w=128)[:, :, 76:128]
                instr = e.memset(ph, 0.0)
            sV.inc(instr)
            for c in range(NT):
                sV.inc(e.tensor_scalar(maskC[c % 2][:], idx_bc[:],
                                       iota2d[:, c:c+1], None, ALU.is_ge))
                e.wait_ge(sV.h, sV.n)
                e.wait_ge(sS.h, s15 + c + 1)
                sV.inc(e.tensor_tensor(attn[c][:], attn[c][:], maskC[c % 2][:], ALU.mult))

        @blk.tensor
        def _(e):
            e.wait_ge(sS.h, s_kT)
            e.wait_ge(sS.h, s_gather)
            e.wait_ge(sV.h, v15 + 1)
            for c in range(NT):
                if c >= 2:
                    e.wait_ge(sS.h, s15 + (c - 2) + 1)
                for j in range(NP):
                    ksrc, jb = (kTa, j) if j < 3 else (kT, j - 3)
                    instr = e.matmul(pS15[c % 2][:, 128*j:128*j + 76],
                                     ksrc[:, 2048*jb + 128*c:2048*jb + 128*(c+1)],
                                     qselT[j][:, 0:76], start=True, stop=True)
                sT.inc(instr)
                e.wait_ge(sV.h, v15 + 1 + 2 * c + 2)
                for n in range(2):
                    instr = e.matmul(pRS[0:1, 512*n:512*(n+1)], ones_cb[:],
                                     attn[c][:, 512*n:512*(n+1)],
                                     start=(c == 0), stop=(c == NT - 1))
                sT.inc(instr)

        @blk.scalar
        def _(e):
            for c in range(NT):
                e.wait_ge(sT.h, t15 + 2 * c + 1)
                sS.inc(e.activation(attn[c][:], pS15[c % 2][:], AF.Exp))
        s_attn = sS.n
        t_attn = sT.n
        v_attn = sV.n
    if _stop("attn", [(sS.h, s_attn), (sV.h, v_attn), (sT.h, t_attn)],
             [(attn[c][:], 128 * c) for c in range(NT)]):
        return nc, tap_aps
    rPS1.close()

    if mktap("attn5", [128, 1024], BF16):
        with nc.Block() as blk:
            @blk.sync
            def _(e):
                e.wait_ge(sV.h, v_attn)
                sD.inc(e.dma_start(out=tap_aps["attn5"], in_=attn[5][:]), 16)

    rAV = Region(nc)
    recipb = rAV.sb("recipb", [128, 1024], side="right")
    pAV = [rAV.ps(f"pAV{r}", [128, 128]) for r in range(2)]
    pRB = [rAV.ps(f"pRB{r}", [128, 512]) for r in range(2)]
    if True:
        blk = MBLK
        @blk.vector
        def _(e):
            e.wait_ge(sT.h, t_attn)
            sV.inc(e.reciprocal(rowpk[R_RECIP:R_RECIP+1, :], pRS[0:1, :]))
        v_rc = sV.n

        @blk.tensor
        def _(e):
            e.wait_ge(sV.h, v_rc)
            e.wait_ge(sS.h, s_attn)
            for n in range(2):
                sT.inc(e.matmul(pRB[n][:, 0:512], ones_rf[0:1, :],
                                rowpk[R_RECIP:R_RECIP+1, 512*n:512*(n+1)],
                                start=True, stop=True))
        t_rb = sT.n

        @blk.vector
        def _(e):
            e.wait_ge(sT.h, t_rb)
            e.tensor_copy(recipb[:, 0:512], pRB[0][:, 0:512])
            sV.inc(e.tensor_copy(recipb[:, 512:1024], pRB[1][:, 0:512]))
            e.wait_ge(sV.h, sV.n)
        v_rb = sV.n

        @blk.tensor
        def _(e):
            e.wait_ge(sS.h, s_vbf)
            for j in range(NP):
                if j >= 2:
                    e.wait_ge(sV.h, v_rb + (j - 2) + 1)
                for c in range(NT):
                    instr = e.matmul(pAV[j % 2][:, 0:76],
                                     vbf[:, 1024*c + 128*j:1024*c + 128*(j+1)],
                                     attn[c][:, 128*j:128*j + 76],
                                     start=(c == 0), stop=(c == NT - 1))
                sT.inc(instr)
        t_av = sT.n - NP

        @blk.vector
        def _(e):
            for j in range(NP):
                e.wait_ge(sT.h, t_av + j + 1)
                e.tensor_tensor(soT[j][0:64, 0:38], pAV[j % 2][0:64, 0:38],
                                recipb[0:64, 128*j:128*j+38], ALU.mult)
                sV.inc(e.tensor_tensor(soT[j][64:128, 38:76], pAV[j % 2][64:128, 38:76],
                                       recipb[64:128, 128*j+38:128*j+76], ALU.mult))
        v_soT = sV.n
    rAV.close()
    rPS2.close()
    rAT.close()
    rKT.close()

    rIO = Region(nc)
    iota1k = rIO.sb("iota1k", [128, 1024])
    rWO = Region(nc)
    wob = rWO.sb("wob", [128, NI * 1024], BF16, side="right")
    if True:
        blk = MBLK
        @blk.gpsimd
        def _(e):
            e.wait_ge(sV.h, v_soT)
            kw = dict(allow_small_or_imprecise_dtypes=True)
            sW.inc(e.iota(iota1k[:], [[1, 1024]], base=0, channel_multiplier=0, **kw))
            for i in range(NI):
                sG.inc(e.dma_start(out=wob[:, 1024*i:1024*(i+1)],
                                   in_=w_h["Wo"].ap()[128*i:128*(i+1), :]), 16)
        g_wo = sG.n
        g_io = sW.n

    if _stop("soT", [(sV.h, v_soT)],
             [(soT[j][:], 128 * j) for j in range(NP)] + [(recipb[:], 1024)]):
        return nc, tap_aps

    if mktap("soT0", [128, 76], BF16):
        with nc.Block() as blk:
            @blk.sync
            def _(e):
                e.wait_ge(sV.h, v_soT)
                sD.inc(e.dma_start(out=tap_aps["soT0"], in_=soT[0][:]), 16)

    rBO = Region(nc)
    bo_bc = rBO.sb("bo_bc", [128, 1024])
    rBOp = Region(nc)
    pBO = [rBOp.ps(f"pBO{r}", [128, 512]) for r in range(2)]
    if True:
        blk = MBLK
        @blk.tensor
        def _(e):
            e.wait_ge(sV.h, v_soT)
            for n in range(2):
                sT.inc(e.matmul(pBO[n][:, 0:512], ones_rf[R_BO:R_BO+1, :],
                                rowpk[R_BO:R_BO+1, 512*n:512*(n+1)],
                                start=True, stop=True))
        t_bo = sT.n

        @blk.vector
        def _(e):
            e.wait_ge(sT.h, t_bo)
            e.tensor_copy(bo_bc[:, 0:512], pBO[0][:, 0:512])
            sV.inc(e.tensor_copy(bo_bc[:, 512:1024], pBO[1][:, 0:512]))
        v_bo = sV.n
    rBOp.close()

    rGH = Region(nc)
    G = [rGH.sb(f"G{j}", [128, 1024], BF16) for j in range(NP)]
    r17 = Region(nc)
    pG = [r17.ps(f"pG{r}", [128, 1024]) for r in range(2)]
    if True:
        blk = MBLK
        t17, s17 = sT.n, sS.n

        @blk.gpsimd
        def _(e):
            e.wait_ge(sV.h, v_soT)
            for j in range(NP):
                instr = e.memset(G[j][:], 0.0)
            sW.inc(instr)
        g_gz = sW.n

        @blk.tensor
        def _(e):
            e.wait_ge(sG.h, g_wo)
            e.wait_ge(sV.h, v_bo)
            for j in range(NP):
                e.wait_ge(sV.h, v_soT - NP + j + 1)
                if j >= 2:
                    e.wait_ge(sS.h, s17 + (j - 2) + 1)
                for s in range(2):
                    for n in range(2):
                        instr = e.matmul(
                            pG[j % 2][64*s:64*s + 38, 512*n:512*(n+1)],
                            soT[j][64*s:64*s + 64, 38*s:38*s + 38],
                            wob[64*s:64*s + 64, 1024*j + 512*n:1024*j + 512*(n+1)],
                            start=True, stop=True)
                sT.inc(instr)

        @blk.scalar
        def _(e):
            e.wait_ge(sW.h, g_gz)
            for j in range(NP):
                e.wait_ge(sT.h, t17 + j + 1)
                e.copy(G[j][0:38, :], pG[j % 2][0:38, :])
                sS.inc(e.copy(G[j][64:102, :], pG[j % 2][64:102, :]))
        s_G = sS.n
    r17.close()
    rWO.close()
    rVB.close()
    rKTa.close()
    rXT.close()

    if _stop("G", [(sS.h, s_G)], [(G[j][:], 128 * j) for j in range(NP)]):
        return nc, tap_aps

    if mktap("G0", [128, 1024], BF16):
        with nc.Block() as blk:
            @blk.sync
            def _(e):
                e.wait_ge(sS.h, s_G)
                sD.inc(e.dma_start(out=tap_aps["G0"], in_=G[0][:]), 16)

    rP = Region(nc)
    P = [rP.sb(f"P{k}", [128, 2048], BF16) for k in range(NP)]
    rSG = Region(nc)
    scG = [rSG.sb(f"scG{m}", [128, 1024], BF16) for m in range(NT)]
    rSGp = Region(nc)
    pSCG = [rSGp.ps(f"pSCG{r}", [128, 1024]) for r in range(2)]
    s_scg = {}
    if True:
        blk = MBLK
        @blk.vector
        def _(e):
            e.wait_ge(sW.h, g_io)
            e.wait_ge(sV.h, v_idxc)
            for h in range(2):
                for k in range(NP):
                    instr = e.tensor_scalar(P[k][:, 1024*h:1024*(h+1)],
                                            iota1k[:], pf(1024.0 * h),
                                            idxcol[:, k:k+1], ALU.add, ALU.is_equal)
                sV.inc(instr)
        v_P0 = sV.n - 2

    if True:
        blk = MBLK
        t_pre = sT.n

        def tsg_pre(m):
            return t_pre + 2 * (m + 1)

        @blk.scalar
        def _(e):
            for m in range(6):
                e.wait_ge(sT.h, tsg_pre(m))
                sS.inc(e.copy(scG[m][:], pSCG[m % 2][:]))
                s_scg[m] = sS.n

        @blk.tensor
        def _(e):
            e.wait_ge(sS.h, s_G)
            e.wait_ge(sV.h, v_P0 + 1)
            for m in range(6):
                if m >= 2:
                    e.wait_ge(sS.h, s_scg[m - 2])
                for n in range(2):
                    for k in range(NP):
                        instr = e.matmul(pSCG[m % 2][:, 512*n:512*(n+1)],
                                         P[k][:, 128*m:128*(m+1)],
                                         G[k][:, 512*n:512*(n+1)],
                                         start=(k == 0), stop=(k == NP - 1))
                    sT.inc(instr)

    rSN = Region(nc)
    signT = rSN.sb("signT", [128, NI * 1024], BF16, side="right")
    mkb = rSN.sb("mkb", [128, 1024], BF16, side="right")
    sgn = [rSN.sb(f"sgn{r}", [128, 1024], BF16, side="right") for r in range(2)]
    wfr = [rSN.sb(f"wfr{r}", [128, 1024], side="right") for r in range(2)]
    rSNp = Region(nc)
    p18 = rSNp.ps("p18", [1, 8])
    p18c = rSNp.ps("p18c", [128, 1])
    pT18 = [rSNp.ps(f"pT18{r}", [128, 512]) for r in range(2)]

    wf1 = [0] * NI
    if True:
        blk = MBLK
        v18, t18 = sV.n, sT.n

        @blk.sync
        def _(e):
            e.wait_ge(sS.h, s_G)
            for rr in range(NI):
                if rr >= 2:
                    e.wait_ge(sV.h, v18 + (rr - 2) + 1)
                sF[rr % 2].inc(e.dma_start(out=wfr[rr % 2][:],
                                   in_=w_h["Wf"].ap()[128*rr:128*(rr+1), :]), 16)
                wf1[rr] = sF[rr % 2].n

        @blk.vector
        def _(e):
            for rr in range(NI):
                e.wait_ge(sF[rr % 2].h, wf1[rr])
                sV.inc(e.tensor_reduce(wstat[:, rr:rr+1], wfr[rr % 2][:], AX.X,
                                       ALU.add, apply_absolute_value=True))
            e.wait_ge(sV.h, sV.n)
            sV.inc(e.tensor_reduce(SL(S_TMP)[:, 2:3], wstat[:, 0:8], AX.X, ALU.add))
        v_wsum = sV.n

        @blk.tensor
        def _(e):
            e.wait_ge(sV.h, v_wsum)
            sT.inc(e.matmul(p18[0:1, 0:1], SL(S_TMP)[:, 2:3], ones_cf[:], start=True, stop=True))
        t_wtot = sT.n

        @blk.vector
        def _(e):
            e.wait_ge(sT.h, t_wtot)
            sV.inc(e.tensor_scalar(srow[0:1, 3:4], p18[0:1, 0:1],
                                   pf(1.0 / (1024.0 * 1024.0)), pf(0.7),
                                   ALU.mult, ALU.mult))
        v_dl = sV.n

        @blk.tensor
        def _(e):
            e.wait_ge(sV.h, v_dl)
            sT.inc(e.matmul(p18c[:], ones_rf[0:1, :],
                            srow[0:1, 3:4], start=True, stop=True))
        t_dl = sT.n

        @blk.vector
        def _(e):
            e.wait_ge(sT.h, t_dl)
            sV.inc(e.tensor_copy(dcol[:], p18c[:]))
        v_dcol = sV.n

    wf2 = [0] * NI
    if True:
        blk = MBLK
        v18b, s18b, t18b = sV.n, sS.n, sT.n

        tsg, t18d = {}, {}
        _tc = 0
        for _m in range(6, 16):
            _tc += 2
            tsg[_m] = t18b + _tc
            if _m <= 13:
                _rr = _m - 6
                _tc += 1
                t18d[2 * _rr] = t18b + _tc
                _tc += 1
                t18d[2 * _rr + 1] = t18b + _tc

        @blk.sync
        def _(e):
            for rr in range(NI):
                if rr >= 2:
                    e.wait_ge(sV.h, v18b + 3 * (rr - 2) + 3)
                else:
                    e.wait_ge(sV.h, v18 + NI)
                sX[rr % 2].inc(e.dma_start(out=wfr[rr % 2][:],
                                   in_=w_h["Wf"].ap()[128*rr:128*(rr+1), :]), 16)
                wf2[rr] = sX[rr % 2].n

        s_abs_d, s_sign_d, s_evac18 = {}, {}, {}

        def evac_sg(e, g):
            rr, a = g // 2, g % 2
            e.wait_ge(sT.h, t18d[g])
            dst = signT[:].rearrange("p (c o) -> p c o", o=1024)[:, 4*a:4*a+4, 128*rr:128*(rr+1)]
            src = pT18[g % 2][:].rearrange("p (c o) -> p c o", o=128)
            sS.inc(e.copy(dst, src))
            s_evac18[g] = sS.n

        @blk.scalar
        def _(e):
            for rr in range(NI):
                e.wait_ge(sX[rr % 2].h, wf2[rr])
                if rr == 0:
                    e.wait_ge(sV.h, v_dcol)
                if rr >= 1:
                    e.wait_ge(sV.h, v18b + 3 * rr)
                sS.inc(e.activation(fscr[0][:], wfr[rr % 2][:], AF.Abs))
                s_abs_d[rr] = sS.n
                e.wait_ge(sV.h, v18b + 3 * rr + 2)
                sS.inc(e.activation(fscr[0][:], wfr[rr % 2][:], AF.Sign))
                s_sign_d[rr] = sS.n
                if rr >= 1:
                    for a in range(2):
                        evac_sg(e, 2 * (rr - 1) + a)
                m = 6 + rr
                e.wait_ge(sT.h, tsg[m])
                sS.inc(e.copy(scG[m][:], pSCG[m % 2][:]))
                s_scg[m] = sS.n
            for a in range(2):
                evac_sg(e, 2 * (NI - 1) + a)
            for m in (14, 15):
                e.wait_ge(sT.h, tsg[m])
                sS.inc(e.copy(scG[m][:], pSCG[m % 2][:]))
                s_scg[m] = sS.n
        s_signT = s_evac18[2 * NI - 1]
        s_scg_all = s_scg[NT - 1]

        @blk.vector
        def _(e):
            e.wait_ge(sV.h, v_dcol)
            for rr in range(NI):
                e.wait_ge(sS.h, s_abs_d[rr])
                sV.inc(e.scalar_tensor_tensor(fscr[1][:], fscr[0][:], dcol[:],
                                              fscr[0][:], ALU.is_gt, ALU.mult,
                                              accum_out=wstat[:, 8+rr:9+rr]))
                sV.inc(e.tensor_scalar(mkb[:], fscr[0][:], dcol[:], None, ALU.is_gt,
                                       ALU.add, accum_out=wstat[:, 16+rr:17+rr]))
                e.wait_ge(sS.h, s_sign_d[rr])
                if rr >= 2:
                    e.wait_ge(sT.h, t18d[2 * (rr - 2) + 1])
                sV.inc(e.tensor_tensor(sgn[rr % 2][:], fscr[0][:], mkb[:], ALU.mult))

        @blk.tensor
        def _(e):
            e.wait_ge(sV.h, v_P0 + 2)
            for m in range(6, 16):
                e.wait_ge(sS.h, s_scg[m - 2])
                for n in range(2):
                    for k in range(NP):
                        instr = e.matmul(pSCG[m % 2][:, 512*n:512*(n+1)],
                                         P[k][:, 128*m:128*(m+1)],
                                         G[k][:, 512*n:512*(n+1)],
                                         start=(k == 0), stop=(k == NP - 1))
                    sT.inc(instr)
                if m <= 13:
                    rr = m - 6
                    e.wait_ge(sV.h, v18b + 3 * rr + 3)
                    for a in range(2):
                        g = 2 * rr + a
                        if g >= 2:
                            e.wait_ge(sS.h, s_evac18[g - 2])
                        for i in range(4):
                            ic = 4 * a + i
                            instr = e.matmul(pT18[g % 2][:, 128*i:128*(i+1)],
                                             sgn[rr % 2][:, 128*ic:128*(ic+1)],
                                             i128b[:], start=True, stop=True)
                            if i == 3:
                                sT.inc(instr)

    if True:
        blk = MBLK
        v18c = sV.n

        @blk.vector
        def _(e):
            e.tensor_copy(fscr[0][:, 2:3], fscr[1][:, 0:1])
            sV.inc(e.tensor_copy(fscr[0][:, 3:4], fscr[1][:, 1:2]))
            e.wait_ge(sV.h, sV.n)
            sV.inc(e.tensor_reduce(SL(S_TMP)[:, 0:1], wstat[:, 8:16], AX.X, ALU.add))
            sV.inc(e.tensor_reduce(SL(S_TMP)[:, 1:2], wstat[:, 16:24], AX.X, ALU.add))
            e.wait_ge(sV.h, sV.n)
        v_al = sV.n

        @blk.tensor
        def _(e):
            e.wait_ge(sV.h, v_al)
            e.matmul(p18[0:1, 1:2], SL(S_TMP)[:, 0:1], ones_cf[:], start=True, stop=True)
            sT.inc(e.matmul(p18[0:1, 2:3], SL(S_TMP)[:, 1:2], ones_cf[:], start=True, stop=True))
        t_al = sT.n

        @blk.vector
        def _(e):
            e.wait_ge(sT.h, t_al)
            sV.inc(e.tensor_scalar(srow[0:1, 4:5], p18[0:1, 2:3], pf(1.0), None, ALU.max))
            e.wait_ge(sV.h, sV.n)
            sV.inc(e.reciprocal(srow[0:1, 5:6], srow[0:1, 4:5]))
            sV.inc(e.tensor_copy(fscr[0][0:1, 0:1], p18[0:1, 1:2]))
            e.wait_ge(sV.h, sV.n)
            sV.inc(e.tensor_tensor(srow[0:1, 6:7], fscr[0][0:1, 0:1],
                                   srow[0:1, 5:6], ALU.mult))
            e.wait_ge(sV.h, sV.n)
            sV.inc(e.tensor_tensor(srow[0:1, 7:8], srow[0:1, 6:7],
                                   srow[0:1, 1:2], ALU.mult))
        v_sa = sV.n

        @blk.tensor
        def _(e):
            e.wait_ge(sV.h, v_sa)
            sT.inc(e.matmul(p18c[:], ones_rf[0:1, :],
                            srow[0:1, 7:8], start=True, stop=True))
        t_sa = sT.n

        @blk.vector
        def _(e):
            e.wait_ge(sT.h, t_sa)
            sV.inc(e.tensor_copy(sa_col[:], p18c[:]))
        v_sacol = sV.n

    if _stop("sign", [(sS.h, s_signT), (sV.h, v_sacol)],
             [(signT[:, 1024*k:1024*(k+1)], 128 * k) for k in range(NI)] +
             [(srow[:], 1536), (dcol[:], 1537), (sa_col[:], 1538)]):
        return nc, tap_aps

    rSNp.close()
    rSGp.close()
    rFN = Region(nc)
    ffnr = [rFN.sb(f"ffnr{r}", [128, 1024]) for r in range(3)]
    r21 = Region(nc)
    x2r = [r21.sb(f"x2r{r}", [128, 1024]) for r in range(2)]
    outr = [r21.sb(f"outr{r}", [128, 1024]) for r in range(2)]
    r19 = Region(nc)
    pF = [r19.ps(f"pF{r}", [128, 512]) for r in range(4)]
    x2d = [0] * NT
    outd = [0] * NT
    if True:
        blk = MBLK
        t19, s19, v21, w21 = sT.n, sS.n, sV.n, sW.n

        @blk.sync
        def _(e):
            e.wait_ge(sS.h, s_G)
            for m in range(NT):
                if m >= 2:
                    e.wait_ge(sV.h, v21 + 2 * (m - 2) + 2)
                sX[m % 2].inc(e.dma_start(out=x2r[m % 2][:], in_=x_ap[128*m:128*(m+1), :]), 16)
                x2d[m] = sX[m % 2].n
                if m >= 2:
                    e.wait_ge(sV.h, v21 + 2 * (m - 2) + 2)
                    sO[m % 2].inc(e.dma_start(out=y_ap[128*(m-2):128*(m-1), :],
                                       in_=outr[(m - 2) % 2][:]), 16)
                    outd[m - 2] = sO[m % 2].n
            for m in (NT - 2, NT - 1):
                e.wait_ge(sV.h, v21 + 2 * m + 2)
                sO[m % 2].inc(e.dma_start(out=y_ap[128*m:128*(m+1), :], in_=outr[m % 2][:]), 16)
                outd[m] = sO[m % 2].n
            e.wait_ge(sO[0].h, sO[0].n)
            e.wait_ge(sO[1].h, sO[1].n)
            e.wait_ge(sD.h, sD.n)

        @blk.gpsimd
        def _(e):
            e.wait_ge(sV.h, v_bo)
            for m in range(NT):
                e.wait_ge(sX[m % 2].h, x2d[m])
                sW.inc(e.tensor_tensor(x2r[m % 2][:], x2r[m % 2][:], bo_bc[:], ALU.add))

        @blk.tensor
        def _(e):
            e.wait_ge(sS.h, s_signT)
            e.wait_ge(sV.h, v_sacol)
            e.wait_ge(sV.h, v_qiT)
            e.wait_ge(sS.h, s_scg_all)
            for m in range(NT):
                for n in range(2):
                    g = 2 * m + n
                    if g >= 4:
                        e.wait_ge(sS.h, s19 + (g - 4) + 1)
                    for ic in range(NI):
                        instr = e.matmul(pF[g % 4][:, 0:512],
                                         qiT[:, 2048*ic + 128*m:2048*ic + 128*(m+1)],
                                         signT[:, 1024*ic + 512*n:1024*ic + 512*(n+1)],
                                         start=(ic == 0), stop=(ic == NI - 1))
                    sT.inc(instr)

        @blk.scalar
        def _(e):
            for g in range(2 * NT):
                m, n = g // 2, g % 2
                e.wait_ge(sT.h, t19 + g + 1)
                if m >= 3 and n == 0:
                    e.wait_ge(sV.h, v21 + 2 * (m - 3) + 2)
                sS.inc(e.copy(ffnr[m % 3][:, 512*n:512*(n+1)], pF[g % 4][:, 0:512]))

        @blk.vector
        def _(e):
            for m in range(NT):
                e.wait_ge(sS.h, s19 + 2 * m + 2)
                e.wait_ge(sW.h, w21 + m + 1)
                if m >= 2:
                    e.wait_ge(sO[m % 2].h, outd[m - 2] if outd[m - 2] else 0)
                sV.inc(e.scalar_tensor_tensor(fscr[1][:], ffnr[m % 3][:], sa_col[:],
                                              x2r[m % 2][:], ALU.mult, ALU.add))
                e.wait_ge(sV.h, sV.n)
                sV.inc(e.tensor_tensor(outr[m % 2][:], fscr[1][:], scG[m][:], ALU.add))

    r19.close()
    r21.close()
    rFN.close()
    rSN.close()
    rSG.close()
    rP.close()
    rGH.close()
    rBO.close()
    rIO.close()
    st.close()
    return nc, tap_aps


_IN_ORDER = ["x", "Wq", "bq", "Wk", "bk", "Wv", "bv", "Wo", "bo", "Wf", "gamma", "beta"]


def kernel(L0_attn_x, L0_attn_Wq, L0_attn_bq, L0_attn_Wk, L0_attn_bk,
           L0_attn_Wv, L0_attn_bv, L0_attn_Wo, L0_attn_bo,
           L0_ffn_weight, L0_ffn_gamma, L0_ffn_beta):
    x = np.ascontiguousarray(np.asarray(L0_attn_x, dtype=np.float32))
    shared = {
        "Wq": np.ascontiguousarray(np.asarray(L0_attn_Wq, dtype=np.float32)),
        "bq": np.ascontiguousarray(np.asarray(L0_attn_bq, dtype=np.float32)),
        "Wk": np.ascontiguousarray(np.asarray(L0_attn_Wk, dtype=np.float32)),
        "bk": np.ascontiguousarray(np.asarray(L0_attn_bk, dtype=np.float32)),
        "Wv": np.ascontiguousarray(np.asarray(L0_attn_Wv, dtype=np.float32)),
        "bv": np.ascontiguousarray(np.asarray(L0_attn_bv, dtype=np.float32)),
        "Wo": np.ascontiguousarray(np.asarray(L0_attn_Wo, dtype=np.float32)),
        "bo": np.ascontiguousarray(np.asarray(L0_attn_bo, dtype=np.float32)),
        "Wf": np.ascontiguousarray(np.asarray(L0_ffn_weight, dtype=np.float32)),
        "gamma": np.ascontiguousarray(np.asarray(L0_ffn_gamma, dtype=np.float32)),
        "beta": np.ascontiguousarray(np.asarray(L0_ffn_beta, dtype=np.float32)),
    }
    nc, _ = build_nc()
    in_maps = [dict(shared, x=np.ascontiguousarray(x[b])) for b in range(8)]
    res = run_bass_kernel_spmd(nc, in_maps, core_ids=list(range(8)))
    return np.stack([np.asarray(res.results[b]["y"]) for b in range(8)], axis=0)
